# version 20
# baseline (speedup 1.0000x reference)
import numpy as np
from contextlib import ExitStack

import concourse.bass as bass
import concourse.mybir as mybir
from concourse.bass_utils import run_bass_kernel_spmd
from concourse.alu_op_type import AluOpType as Op

F32 = mybir.dt.float32
BF16 = mybir.dt.bfloat16
AF = mybir.ActivationFunctionType

T = 2048
D = 4096
NK = 32
CCH = 8
E = 8
TC = 256
NCH = T // TC
TH = 1024
TC5 = 512
OS = 128
NOS = D // OS
NOCH = NK


class Walker:

    ENGINES = ("tensor", "vector", "scalar", "sync", "gpsimd")

    def __init__(self, nc, sems):
        self.nc = nc
        self.sems = sems
        self.cnt = {k: 0 for k in sems}
        self.ev = {}
        self.mode = "count"
        self.handle = None

    AUTO = {"vector": "ve", "scalar": "sc"}

    def I(self, eng, build_fn, incs=(), key=None, noauto=False):
        auto = None if noauto else self.AUTO.get(eng)
        prev = None
        extra = list(incs)
        if auto is not None:
            prev = self.cnt[auto]
            self.cnt[auto] += 1
            if key is not None:
                self.ev[(key, auto)] = self.cnt[auto]
            extra = [(s, b) for (s, b) in incs if s != auto]
        for sem, by in extra:
            self.cnt[sem] += by
            if key is not None:
                self.ev[(key, sem)] = self.cnt[sem]
        if self.mode == eng:
            if prev:
                self.handle.wait_ge(self.sems[auto], prev)
            inst = build_fn(self.handle)
            if auto is not None:
                inst = inst.then_inc(self.sems[auto], 1)
            for sem, by in extra:
                inst = inst.then_inc(self.sems[sem], by)

    def W(self, eng, sem, key):
        if self.mode == eng:
            v = self.ev.get((key, sem))
            assert v is not None, f"wait on unknown event {key}/{sem}"
            self.handle.wait_ge(self.sems[sem], v)

    def Wv(self, eng, sem, value):
        if self.mode == eng:
            self.handle.wait_ge(self.sems[sem], value)

    def run_block(self, phase_fn):
        snap = dict(self.cnt)
        self.mode = "count"
        self.handle = None
        phase_fn(self)
        final = dict(self.cnt)
        with self.nc.Block(no_gpsimd_drain=True) as block:
            for eng in self.ENGINES:
                def make(eng):
                    def _(h):
                        self.cnt = dict(snap)
                        self.mode = eng
                        self.handle = h
                        phase_fn(self)
                    return _
                getattr(block, eng)(make(eng))
        self.cnt = final
        self.mode = "count"
        self.handle = None


def build_program(stop='full'):
    nc = bass.Bass("TRN2", target_bir_lowering=False)

    xT = nc.dram_tensor("xT", [D, T], F32, kind="ExternalInput").ap()
    wqselT = nc.dram_tensor("wqselT", [D, CCH * 128], F32, kind="ExternalInput").ap()
    bqsel = nc.dram_tensor("bqsel", [128, CCH], F32, kind="ExternalInput").ap()
    woT = nc.dram_tensor("woT", [D, D], F32, kind="ExternalInput").ap()
    mT = nc.dram_tensor("mT", [E, 128, 128], F32, kind="ExternalInput").ap()
    zin = nc.dram_tensor("zin", [E * 128, 1], F32, kind="ExternalInput").ap()
    gwT = nc.dram_tensor("gwT", [D, E], F32, kind="ExternalInput").ap()
    sel8in = nc.dram_tensor("sel8in", [E, E * 128], F32, kind="ExternalInput").ap()
    incs = nc.dram_tensor("incs", [E, D], F32, kind="ExternalInput").ap()
    outcs = nc.dram_tensor("outcs", [E, D], F32, kind="ExternalInput").ap()
    biasc = nc.dram_tensor("biasc", [128, NK], F32, kind="ExternalInput").ap()
    wbT = nc.dram_tensor("wbT", [D, D], F32, kind="ExternalInput").ap()
    outT = nc.dram_tensor("outT", [D, T], F32, kind="ExternalOutput").ap()
    x1sc = nc.dram_tensor("x1sc", [D, T], BF16, kind="Internal").ap()

    xT3 = xT.rearrange("(K p) t -> K p t", p=128)
    wq3 = wqselT.rearrange("(K p) c -> K p c", p=128)
    wb3 = wbT.rearrange("(K p) o -> K p o", p=128)
    x13 = x1sc.rearrange("(K p) t -> K p t", p=128)

    ctx = ExitStack()
    sems = {}
    for name in ("mm", "ve", "sc", "gp", "pre_g", "wq", "wo", "wo2", "syc",
                 "xa", "xb", "x1a", "x1b", "wba", "wbb", "wbc",
                 "xra", "xrb", "ya", "yb", "ra", "rb", "oa", "ob",
                 "oc", "od", "oe"):
        sems[name] = ctx.enter_context(nc.semaphore(name))
    w = Walker(nc, sems)

    ps = [ctx.enter_context(nc.psum_tensor(f"ps{i}", [128, 512], F32))
          for i in range(8)]

    sb = lambda name, shape, dt: ctx.enter_context(nc.sbuf_tensor(name, shape, dt))
    bq_sb = sb("bq_sb", [128, CCH], F32)
    negbq = sb("negbq", [128, CCH], F32)
    zbd = sb("zbd", [128, E * E], BF16)
    gwstat = sb("gwstat", [128, NK * 8], BF16)
    sel8 = sb("sel8", [8, E * 128], BF16)
    bias_sb = sb("bias_sb", [128, NK], F32)
    ones128 = sb("ones128", [128, 1], BF16)
    ones1 = sb("ones1", [1, 128], BF16)
    gwsneg = sb("gwsneg", [8, 1], F32)
    rtg = sb("rtg", [8, T], BF16)

    scope_a12 = ExitStack()
    ssig = scope_a12.enter_context(nc.sbuf_tensor("ssig", [128, CCH * T], BF16))


    def _dump_and_stop(tile, rows, kchunks):
        def fin(w):
            w.I("gpsimd", lambda g: g.dma_start(
                outT.rearrange("(K p) t -> K p t", p=128)[0:kchunks]
                .rearrange("K p t -> p K t"),
                tile[:].rearrange("p (K t) -> p K t", K=kchunks)),
                [("pre_g", 16)], "fin.dump")
            w.W("gpsimd", "pre_g", "fin.dump")
        w.run_block(fin)
        return nc
    scope_a1 = ExitStack()
    wqsel_sb = scope_a1.enter_context(nc.sbuf_tensor("wqsel", [128, NK * 1024], BF16))
    x_bf = [scope_a1.enter_context(nc.sbuf_tensor(f"x_bf{i}", [128, NK * TC], BF16))
            for i in range(2)]
    sig_sb = [scope_a1.enter_context(nc.sbuf_tensor(f"sig{i}", [128, CCH * TC], BF16))
              for i in range(2)]
    tmpA = [scope_a1.enter_context(nc.sbuf_tensor(f"tmpA{i}", [128, TC], BF16))
            for i in range(2)]
    tmpB = [scope_a1.enter_context(nc.sbuf_tensor(f"tmpB{i}", [128, TC], BF16))
            for i in range(2)]
    dentmp = scope_a1.enter_context(nc.sbuf_tensor("dentmp", [8, TC], F32))
    rdenf = scope_a1.enter_context(nc.sbuf_tensor("rdenf", [8, TC], F32))
    rdenbf = scope_a1.enter_context(nc.sbuf_tensor("rdenbf", [8, TC], BF16))

    def phase_a1(w):
        w.I("scalar", lambda s: s.preload_act_table(AF.Exp), [("sc", 1)])
        w.I("vector", lambda v: v.memset(ones128[:, :], 1.0), [("ve", 1)], "pre.ones")
        w.I("vector", lambda v: v.memset(ones1[:, :], 1.0), [("ve", 1)])
        w.I("vector", lambda v: v.memset(zbd[:, :], 0.0), [("ve", 1)], "pre.zbd0")
        w.I("sync", lambda s: s.dma_start(bq_sb[:, :], bqsel), [("syc", 16)], "pre.bq")
        w.I("sync", lambda s: s.dma_start(bias_sb[:, :], biasc), [("syc", 16)], "pre.bias")
        w.W("vector", "syc", "pre.bias")
        w.I("vector", lambda v: v.tensor_scalar_mul(negbq[:, :], bq_sb[:, :], -1.0),
            [("ve", 1)], "pre.negbq")
        w.W("gpsimd", "ve", "pre.zbd0")
        for h in range(E):
            w.I("gpsimd", lambda g, h=h: g.dma_start(
                zbd[:, E * h + h: E * h + h + 1], zin[128 * h: 128 * (h + 1), 0:1]),
                [("pre_g", 16)], f"pre.z.{h}")
        for k in range(NK):
            w.I("gpsimd", lambda g, k=k: g.dma_start(
                gwstat[:, 8 * k: 8 * (k + 1)], gwT[128 * k: 128 * (k + 1), :]),
                [("pre_g", 16)], f"pre.gw.{k}")
        w.I("gpsimd", lambda g: g.dma_start(sel8[:, :], sel8in), [("pre_g", 16)], "pre.allg")
        w.I("gpsimd", lambda g: g.dma_start(
            wqsel_sb[:].rearrange("p (K c) -> p K c", K=NK),
            wq3.rearrange("K p c -> p K c")),
            [("wq", 16)], "pre.wqsel")
        w.W("tensor", "wq", "pre.wqsel")
        w.W("tensor", "pre_g", "pre.allg")
        w.W("tensor", "ve", "pre.ones")
        w.I("tensor", lambda t: [
            t.matmul(ps[2][0:8, 0:1], gwstat[:, 8 * k: 8 * (k + 1)], ones128[:, 0:1],
                     start=(k == 0), stop=(k == NK - 1))
            for k in range(NK)][-1], [("mm", 1)], "pre.gwsum")
        w.W("vector", "mm", "pre.gwsum")
        w.I("vector", lambda v: v.tensor_scalar_mul(gwsneg[:, :], ps[2][0:8, 0:1], -1.0),
            [("ve", 1)], "pre.gwsneg")

        def stage_den_a(w, cc):
            if cc >= 1:
                w.W("tensor", "ve", f"a1.rden.{cc-1}")
            w.W("tensor", "ve", f"a1.sig.{cc}.{CCH-1}")
            w.I("tensor", lambda t, cc=cc: [
                t.matmul(ps[4][0:8, 0:TC], zbd[:, E * h: E * (h + 1)],
                         sig_sb[cc % 2][:, TC * h: TC * (h + 1)],
                         start=(h == 0), stop=(h == E - 1))
                for h in range(E)][-1], [("mm", 1)], f"a1.den.{cc}")
            w.W("vector", "mm", f"a1.den.{cc}")
            w.I("vector", lambda v: v.tensor_scalar(
                dentmp[:, :], ps[4][0:8, 0:TC], 1e-6, None, Op.add), [("ve", 1)])
            w.I("vector", lambda v: v.reciprocal(rdenf[:, :], dentmp[:, :]),
                [("ve", 1)], f"a1.rden.{cc}")
            w.W("scalar", "ve", f"a1.rden.{cc}")
            if cc >= 1:
                w.W("scalar", "mm", f"a1.bc.{cc-1}.{E-1}")
            w.I("scalar", lambda s: s.copy(rdenbf[:, :], rdenf[:, :]),
                [("sc", 1)], f"a1.rdenbf.{cc}")

        def stage_den_b(w, cc):
            w.W("tensor", "sc", f"a1.rdenbf.{cc}")
            for h in range(E):
                gidx = cc * E + h
                if gidx == 0:
                    w.W("tensor", "ve", "pre.gwsneg")
                if gidx >= 2:
                    pc, ph = divmod(gidx - 2, E)
                    w.W("tensor", "ve", f"a1.ssig.{pc}.{ph}")
                w.I("tensor", lambda t, h=h: t.matmul(
                    ps[2 + h % 2][:, 0:TC], sel8[:, 128 * h: 128 * (h + 1)],
                    rdenbf[0:8, 0:TC],
                    start=True, stop=True), [("mm", 1)], f"a1.bc.{cc}.{h}")
            for h in range(E):
                w.W("vector", "mm", f"a1.bc.{cc}.{h}")
                w.I("vector", lambda v, cc=cc, h=h: v.tensor_tensor(
                    ssig[:, T * h + TC * cc: T * h + TC * (cc + 1)],
                    sig_sb[cc % 2][:, TC * h: TC * (h + 1)],
                    ps[2 + h % 2][:, 0:TC], Op.mult), [("ve", 1)], f"a1.ssig.{cc}.{h}")

        for c in range(NCH):
            if c >= 2:
                w.W("gpsimd", "mm", f"a1.q.{c-2}.{CCH-1}")
            w.I("gpsimd", lambda g, c=c: g.dma_start(
                x_bf[c % 2][:].rearrange("p (K t) -> p K t", K=NK),
                xT3[:, :, TC * c: TC * (c + 1)].rearrange("K p t -> p K t")),
                [("xa" if c % 2 == 0 else "xb", 16)], f"a1.x.{c}")

            w.W("tensor", "xa" if c % 2 == 0 else "xb", f"a1.x.{c}")
            for ct in range(CCH):
                gidx = c * CCH + ct
                if gidx >= 2:
                    pc, pct = divmod(gidx - 2, CCH)
                    w.W("tensor", "sc", f"a1.sig3.{pc}.{pct}")
                w.I("tensor", lambda t, c=c, ct=ct: [
                    t.matmul(ps[ct % 2][:, 0:TC],
                             wqsel_sb[:, 1024 * k + 128 * ct: 1024 * k + 128 * ct + 128],
                             x_bf[c % 2][:, TC * k: TC * (k + 1)],
                             start=(k == 0), stop=(k == NK - 1))
                    for k in range(NK)][-1], [("mm", 1)], f"a1.q.{c}.{ct}")
                if c >= 1 and ct == 0:
                    stage_den_a(w, c - 1)
                if c >= 1 and ct == 1:
                    stage_den_b(w, c - 1)

            for ct in range(CCH):
                gidx = c * CCH + ct
                if gidx == 0:
                    w.W("scalar", "ve", "pre.negbq")
                w.W("scalar", "mm", f"a1.q.{c}.{ct}")
                if gidx >= 2:
                    pc, pct = divmod(gidx - 2, CCH)
                    w.W("scalar", "ve", f"a1.sig.{pc}.{pct}")
                w.I("scalar", lambda s, ct=ct: s.activation(
                    tmpA[ct % 2][:, :], ps[ct % 2][:, 0:TC], AF.Relu,
                    bias=negbq[:, ct:ct + 1], scale=-1.0), [("sc", 1)])
                w.I("scalar", lambda s, ct=ct: s.activation(
                    tmpB[ct % 2][:, :], ps[ct % 2][:, 0:TC], AF.Relu,
                    bias=bq_sb[:, ct:ct + 1], scale=1.0), [("sc", 1)], f"a1.sig3.{c}.{ct}")
                w.I("scalar", lambda s, ct=ct: s.activation(
                    tmpA[ct % 2][:, :], tmpA[ct % 2][:, :], AF.Exp,
                    bias=0.0, scale=-1.0), [("sc", 1)], f"a1.sigE.{c}.{ct}")
            for ct in range(CCH):
                w.W("vector", "sc", f"a1.sigE.{c}.{ct}")
                if c >= 2 and ct == 0:
                    w.W("vector", "mm", f"a1.den.{c-2}")
                w.I("vector", lambda v, c=c, ct=ct: v.tensor_tensor(
                    sig_sb[c % 2][:, TC * ct: TC * (ct + 1)],
                    tmpA[ct % 2][:, :], tmpB[ct % 2][:, :], Op.add),
                    [("ve", 1)], f"a1.sig.{c}.{ct}")


        stage_den_a(w, NCH - 1)
        stage_den_b(w, NCH - 1)

    w.run_block(phase_a1)
    if stop == 'a1':
        return _dump_and_stop(ssig, 1024, CCH)
    scope_a1.close()

    w2_sb = scope_a12.enter_context(nc.sbuf_tensor("w2", [128, CCH * D], BF16))
    scope_w2 = ExitStack()
    mt_bf = scope_w2.enter_context(nc.sbuf_tensor("mt_bf", [128, E * 128], BF16))
    wo_buf = [scope_w2.enter_context(nc.sbuf_tensor(f"wo{i}", [128, D], BF16))
              for i in range(8)]

    def phase_w2(w):
        w.I("gpsimd", lambda g: g.dma_start(
            mt_bf[:].rearrange("p (h d) -> p h d", h=E),
            mT.rearrange("h e d -> e h d")), [("pre_g", 16)], "w2.mt")
        for h in range(E):
            for g4 in range(4):
                if h >= 2:
                    w.W("gpsimd", "mm", f"w2.mm.{h-2}.7")
                blk = 4 * h + g4
                w.I("gpsimd", lambda g, h=h, blk=blk, g4=g4: g.dma_start(
                    wo_buf[4 * (h % 2) + g4][:, :], woT[128 * blk: 128 * (blk + 1), :]),
                    [("wo" if h % 2 == 0 else "wo2", 16)], f"w2.wo.{h}.{g4}")
            w.W("tensor", "wo" if h % 2 == 0 else "wo2", f"w2.wo.{h}.3")
            if h == 0:
                w.W("tensor", "pre_g", "w2.mt")
            for ot8 in range(8):
                gidx = h * 8 + ot8
                if gidx >= 2:
                    ph, pot = divmod(gidx - 2, 8)
                    w.W("tensor", "sc", f"w2.evac.{ph}.{pot}")
                w.I("tensor", lambda t, h=h, ot8=ot8: [
                    t.matmul(ps[ot8 % 2][:, 0:512], mt_bf[:, 128 * h: 128 * (h + 1)],
                             wo_buf[4 * (h % 2) + g4][:, 512 * ot8: 512 * (ot8 + 1)],
                             start=(g4 == 0), stop=(g4 == 3))
                    for g4 in range(4)][-1], [("mm", 1)], f"w2.mm.{h}.{ot8}")
            for ot8 in range(8):
                w.W("scalar", "mm", f"w2.mm.{h}.{ot8}")
                w.I("scalar", lambda s, h=h, ot8=ot8: s.copy(
                    w2_sb[:, D * h + 512 * ot8: D * h + 512 * (ot8 + 1)],
                    ps[ot8 % 2][:, 0:512]), [("sc", 1)], f"w2.evac.{h}.{ot8}")

    w.run_block(phase_w2)
    if stop == 'w2':
        return _dump_and_stop(w2_sb, 1024, CCH)
    scope_w2.close()

    scope_a2 = ExitStack()
    lraw = scope_a2.enter_context(nc.sbuf_tensor("lraw", [8, T], F32))
    murow = scope_a2.enter_context(nc.sbuf_tensor("murow", [1, T], BF16))
    rstdrow = scope_a2.enter_context(nc.sbuf_tensor("rstdrow", [1, T], BF16))
    rowf = scope_a2.enter_context(nc.sbuf_tensor("rowf", [8, 512], F32))
    rbuf = [scope_a2.enter_context(nc.sbuf_tensor(f"rbuf{i}", [128, NK * TC], F32))
            for i in range(2)]
    x_res = [scope_a2.enter_context(nc.sbuf_tensor(f"xres{i}", [128, 4 * TC], F32))
             for i in range(2)]
    bc1_sb = scope_a2.enter_context(nc.sbuf_tensor("bc1_sb", [128, TC], BF16))
    bc2_sb = scope_a2.enter_context(nc.sbuf_tensor("bc2_sb", [128, TC], BF16))
    NBF = 3
    r1bf = [scope_a2.enter_context(nc.sbuf_tensor(f"r1bf{i}", [128, 2 * TC], BF16))
            for i in range(NBF)]
    sqbf = [scope_a2.enter_context(nc.sbuf_tensor(f"sqbf{i}", [128, 2 * TC], BF16))
            for i in range(NBF)]
    ln1r = scope_a2.enter_context(nc.sbuf_tensor("ln1r", [1, 5 * TC], F32))
    lnbf = scope_a2.enter_context(nc.sbuf_tensor("lnbf", [1, 2 * TC], BF16))
    exbf = scope_a2.enter_context(nc.sbuf_tensor("exbf", [8, 512], BF16))


    NG = 8

    def phase_a2(w):
        w.I("scalar", lambda s: s.preload_act_table(AF.Sqrt), [("sc", 1)])
        def stage_fin(w, cc):
            w.W("tensor", "ve", f"a2.rowbf.{cc}")
            if cc >= 1:
                w.W("tensor", "sc", f"a2.bcev.{cc-1}")
            w.I("tensor", lambda t: t.matmul(
                ps[6][:, 0:TC], ones1[:, 0:128], lnbf[0:1, 0:TC], start=True, stop=True),
                [("mm", 1)])
            w.I("tensor", lambda t: t.matmul(
                ps[7][:, 0:TC], ones1[:, 0:128], lnbf[0:1, TC:2 * TC], start=True, stop=True),
                [("mm", 1)], f"a2.bc.{cc}")

            w.W("scalar", "mm", f"a2.bc.{cc}")
            if cc >= 1:
                w.W("scalar", "gp", f"a2.norm.{cc-1}")
            w.I("scalar", lambda s: s.copy(bc1_sb[:, :], ps[6][:, 0:TC]), [("sc", 1)])
            w.I("scalar", lambda s: s.copy(bc2_sb[:, :], ps[7][:, 0:TC]),
                [("sc", 1)], f"a2.bcev.{cc}")

            bc1b = bass.AP(bc1_sb, 0, [[TC, 128], [0, NK], [1, TC]])
            bc2b = bass.AP(bc2_sb, 0, [[TC, 128], [0, NK], [1, TC]])
            rv = rbuf[cc % 2][:].rearrange("p (K t) -> p K t", K=NK)
            w.W("gpsimd", "sc", f"a2.bcev.{cc}")
            w.W("gpsimd", "sc", f"a2.sq.{cc}.{NK//2-1}")
            w.I("gpsimd", lambda g, rv=rv: g.tensor_tensor(rv, rv, bc1b, Op.mult),
                [("gp", 1)], f"a2.norm1.{cc}")
            w.W("gpsimd", "gp", f"a2.norm1.{cc}")
            w.I("gpsimd", lambda g, rv=rv: g.tensor_tensor(rv, rv, bc2b, Op.subtract),
                [("gp", 1)], f"a2.norm.{cc}")
            w.W("gpsimd", "gp", f"a2.norm.{cc}")
            w.I("gpsimd", lambda g, cc=cc: g.dma_start(
                x13[:, :, TC * cc: TC * (cc + 1)].rearrange("K p t -> p K t"),
                rbuf[cc % 2][:].rearrange("p (K t) -> p K t", K=NK)),
                [("x1a" if cc % 2 == 0 else "x1b", 16)], f"a2.x1st.{cc}")

        for c in range(NCH):
            for grp in range(NG):
                gidx = c * NG + grp
                if gidx >= 2:
                    pc, pg = divmod(gidx - 2, NG)
                    w.W("sync", "ve", f"a2.r1.{pc}.{4*pg+3}")
                w.I("sync", lambda s, c=c, grp=grp: s.dma_start(
                    x_res[grp % 2][:].rearrange("p (f t) -> p f t", f=4),
                    xT3[4 * grp: 4 * grp + 4, :, TC * c: TC * (c + 1)]
                    .rearrange("f p t -> p f t")),
                    [("xra" if grp % 2 == 0 else "xrb", 16)], f"a2.xres.{c}.{grp}")

            A2B = (0, 1, 3)
            for ot in range(NK):
                gidx = c * NK + ot
                if gidx >= 3:
                    pc, pot = divmod(gidx - 3, NK)
                    w.W("tensor", "ve", f"a2.r1.{pc}.{pot}")
                if ot == 0 and c >= 1:
                    w.W("tensor", "ve", f"a2.strd.{c-1}")
                w.I("tensor", lambda t, c=c, ot=ot: [
                    t.matmul(ps[A2B[ot % 3]][:, 0:TC],
                             w2_sb[:, D * ch + 128 * ot: D * ch + 128 * ot + 128],
                             ssig[:, T * ch + TC * c: T * ch + TC * (c + 1)],
                             start=(ch == 0), stop=(ch == CCH - 1))
                    for ch in range(CCH)][-1], [("mm", 1)], f"a2.at.{c}.{ot}")
                if ot == 6 and c >= 1:
                    stage_fin(w, c - 1)
                STL = 7
                st_pairs = []
                if ot >= STL and (ot - STL) % 2 == 1:
                    st_pairs.append((ot - STL) // 2)
                if ot == NK - 1:
                    st_pairs.extend(range((NK - STL) // 2, NK // 2))
                for sp in st_pairs:
                    w.W("tensor", "sc", f"a2.sq.{c}.{sp}")
                    for sh in range(2):
                        w.I("tensor", lambda t, sp=sp, sh=sh: t.matmul(
                            ps[4][0:8, TC * sh: TC * (sh + 1)],
                            gwstat[:, 8 * (2 * sp + sh): 8 * (2 * sp + sh) + 8],
                            r1bf[sp % NBF][:, TC * sh: TC * (sh + 1)],
                            start=(sp == 0 and sh == 0),
                            stop=(sp == NK // 2 - 1 and sh == 1)), [("mm", 1)])
                    w.I("tensor", lambda t, sp=sp: t.matmul(
                        ps[2][0:1, 0:2 * TC], ones128[:, 0:1], r1bf[sp % NBF][:, :],
                        start=(sp == 0), stop=(sp == NK // 2 - 1)), [("mm", 1)])
                    w.I("tensor", lambda t, sp=sp: t.matmul(
                        ps[5][0:1, 0:2 * TC], ones128[:, 0:1], sqbf[sp % NBF][:, :],
                        start=(sp == 0), stop=(sp == NK // 2 - 1)),
                        [("mm", 1)], f"a2.st.{c}.{sp}")

            if c >= 2:
                w.W("vector", "x1a" if c % 2 == 0 else "x1b", f"a2.x1st.{c-2}")
            for ot in range(NK):
                w.W("vector", "mm", f"a2.at.{c}.{ot}")
                w.W("vector", "xra" if (ot // 4) % 2 == 0 else "xrb", f"a2.xres.{c}.{ot // 4}")
                w.I("vector", lambda v, c=c, ot=ot: v.tensor_tensor(
                    rbuf[c % 2][:, TC * ot: TC * (ot + 1)], ps[A2B[ot % 3]][:, 0:TC],
                    x_res[(ot // 4) % 2][:, TC * (ot % 4): TC * (ot % 4 + 1)], Op.add),
                    [("ve", 1)], f"a2.r1.{c}.{ot}")

            for p in range(NK // 2):
                w.W("scalar", "ve", f"a2.r1.{c}.{2*p+1}")
                pgidx = c * (NK // 2) + p
                if pgidx >= NBF:
                    pc, pp = divmod(pgidx - NBF, NK // 2)
                    w.W("scalar", "mm", f"a2.st.{pc}.{pp}")
                w.I("scalar", lambda s, c=c, p=p: s.copy(
                    r1bf[p % NBF][:, :], rbuf[c % 2][:, 2 * TC * p: 2 * TC * (p + 1)]),
                    [("sc", 1)])
                w.I("scalar", lambda s, p=p: s.square(
                    sqbf[p % NBF][:, :], r1bf[p % NBF][:, :]), [("sc", 1)], f"a2.sq.{c}.{p}")

            MU, VAR, TMP, NU, RSD = 0, TC, 2 * TC, 3 * TC, 4 * TC
            w.W("scalar", "mm", f"a2.st.{c}.{NK//2-1}")
            w.I("scalar", lambda s: s.mul(
                ln1r[0:1, MU:MU + TC], ps[2][0:1, TC:2 * TC], 1.0 / D), [("sc", 1)])
            w.I("scalar", lambda s: s.mul(
                ln1r[0:1, VAR:VAR + TC], ps[5][0:1, TC:2 * TC], 1.0 / D), [("sc", 1)])
            w.I("scalar", lambda s, c=c: s.copy(
                lraw[:, TC * c: TC * (c + 1)], ps[4][0:8, TC:2 * TC]),
                [("sc", 1)], f"a2.stev.{c}")
            w.W("vector", "sc", f"a2.stev.{c}")
            w.I("vector", lambda v: v.scalar_tensor_tensor(
                ln1r[0:1, MU:MU + TC], ps[2][0:1, 0:TC], 1.0 / D,
                ln1r[0:1, MU:MU + TC], Op.mult, Op.add), [("ve", 1)])
            w.I("vector", lambda v: v.scalar_tensor_tensor(
                ln1r[0:1, VAR:VAR + TC], ps[5][0:1, 0:TC], 1.0 / D,
                ln1r[0:1, VAR:VAR + TC], Op.mult, Op.add), [("ve", 1)])
            w.I("vector", lambda v, c=c: v.tensor_tensor(
                lraw[:, TC * c: TC * (c + 1)], ps[4][0:8, 0:TC],
                lraw[:, TC * c: TC * (c + 1)], Op.add), [("ve", 1)], f"a2.strd.{c}")
            w.I("vector", lambda v, c=c: v.tensor_copy(
                murow[0:1, TC * c: TC * (c + 1)], ln1r[0:1, MU:MU + TC]),
                [("ve", 1)], f"a2.mu.{c}")
            w.I("vector", lambda v: v.tensor_tensor(
                ln1r[0:1, TMP:TMP + TC], ln1r[0:1, MU:MU + TC], ln1r[0:1, MU:MU + TC],
                Op.mult), [("ve", 1)])
            w.I("vector", lambda v: v.tensor_tensor(
                ln1r[0:1, VAR:VAR + TC], ln1r[0:1, VAR:VAR + TC], ln1r[0:1, TMP:TMP + TC],
                Op.subtract), [("ve", 1)])
            w.I("vector", lambda v: v.tensor_scalar(
                ln1r[0:1, VAR:VAR + TC], ln1r[0:1, VAR:VAR + TC], 1e-5, None, Op.add),
                [("ve", 1)])
            w.I("vector", lambda v: v.reciprocal(
                ln1r[0:1, TMP:TMP + TC], ln1r[0:1, VAR:VAR + TC]),
                [("ve", 1)], f"a2.rv.{c}")
            w.W("scalar", "ve", f"a2.rv.{c}")
            w.I("scalar", lambda s: s.sqrt(
                ln1r[0:1, RSD:RSD + TC], ln1r[0:1, TMP:TMP + TC]),
                [("sc", 1)], f"a2.rstd.{c}")
            w.I("scalar", lambda s, c=c: s.copy(
                rstdrow[0:1, TC * c: TC * (c + 1)], ln1r[0:1, RSD:RSD + TC]),
                [("sc", 1)], f"a2.rstdbf.{c}")
            w.W("vector", "sc", f"a2.rstd.{c}")
            w.I("vector", lambda v: v.tensor_tensor(
                ln1r[0:1, NU:NU + TC], ln1r[0:1, MU:MU + TC],
                ln1r[0:1, RSD:RSD + TC], Op.mult),
                [("ve", 1)], f"a2.nu.{c}")
            if c >= 1:
                w.W("vector", "mm", f"a2.bc.{c-1}")
            w.I("vector", lambda v: v.tensor_copy(
                lnbf[0:1, 0:TC], ln1r[0:1, RSD:RSD + TC]), [("ve", 1)])
            w.I("vector", lambda v: v.tensor_copy(
                lnbf[0:1, TC:2 * TC], ln1r[0:1, NU:NU + TC]),
                [("ve", 1)], f"a2.rowbf.{c}")


        stage_fin(w, NCH - 1)

        w.I("scalar", lambda s: s.preload_act_table(AF.Exp), [("sc", 1)], "rt.tab")
        for rc in range(T // 512):
            cpos = 512 * rc
            cc = min(2 * rc + 1, NCH - 1)
            w.W("tensor", "ve", f"a2.mu.{cc}")
            w.W("tensor", "sc", f"a2.rstdbf.{cc}")
            if rc == 0:
                w.W("tensor", "ve", f"a2.r1.{NCH-1}.{NK-1}")
            else:
                w.W("tensor", "ve", f"rt.lg.{rc-1}")
            w.I("tensor", lambda t, cpos=cpos: t.matmul(
                ps[0][0:8, 0:512], ones1[:, 0:8], murow[0:1, cpos:cpos + 512],
                start=True, stop=True), [("mm", 1)])
            w.I("tensor", lambda t, cpos=cpos: t.matmul(
                ps[1][0:8, 0:512], ones1[:, 0:8], rstdrow[0:1, cpos:cpos + 512],
                start=True, stop=True), [("mm", 1)], f"rt.bc.{rc}")
            w.W("vector", "mm", f"rt.bc.{rc}")
            w.I("vector", lambda v, cpos=cpos: v.scalar_tensor_tensor(
                rowf[:, :], ps[0][0:8, 0:512], gwsneg[:, 0:1],
                lraw[:, cpos:cpos + 512], Op.mult, Op.add), [("ve", 1)])
            w.I("vector", lambda v: v.tensor_tensor(
                rowf[:, :], rowf[:, :], ps[1][0:8, 0:512], Op.mult),
                [("ve", 1)], f"rt.lg.{rc}")
            w.W("scalar", "ve", f"rt.lg.{rc}")
            w.I("scalar", lambda s: s.activation(exbf[:, :], rowf[:, :], AF.Exp),
                [("sc", 1)], f"rt.ex.{rc}")
            w.W("tensor", "sc", f"rt.ex.{rc}")
            w.I("tensor", lambda t: t.matmul(
                ps[2][0:1, 0:512], ones128[0:8, 0:1], exbf[:, :], start=True, stop=True),
                [("mm", 1)], f"rt.sum.{rc}")
            w.W("vector", "mm", f"rt.sum.{rc}")
            w.I("vector", lambda v: v.reciprocal(rowf[0:1, :], ps[2][0:1, 0:512]),
                [("ve", 1)], f"rt.rs.{rc}")
            w.W("scalar", "ve", f"rt.rs.{rc}")
            w.I("scalar", lambda s: s.copy(lnbf[0:1, 0:512], rowf[0:1, :]),
                [("sc", 1)], f"rt.rsbf.{rc}")
            w.W("tensor", "sc", f"rt.rsbf.{rc}")
            if rc >= 1:
                w.W("tensor", "ve", f"rt.done.{rc-1}")
            w.I("tensor", lambda t: t.matmul(
                ps[3][0:8, 0:512], ones1[:, 0:8], lnbf[0:1, 0:512], start=True, stop=True),
                [("mm", 1)], f"rt.bcr.{rc}")
            w.W("vector", "mm", f"rt.bcr.{rc}")
            w.I("vector", lambda v, cpos=cpos: v.tensor_tensor(
                rtg[:, cpos:cpos + 512], exbf[:, :], ps[3][0:8, 0:512], Op.mult),
                [("ve", 1)], f"rt.done.{rc}")

    w.run_block(phase_a2)
    if stop == 'a2':
        def fin(w):
            w.W("gpsimd", "x1a", f"a2.x1st.{NCH-2}")
            w.W("gpsimd", "x1b", f"a2.x1st.{NCH-1}")
            w.I("gpsimd", lambda g: g.dma_start(outT, x1sc), [("pre_g", 16)], "fin.dump")
            w.W("gpsimd", "pre_g", "fin.dump")
        w.run_block(fin)
        return nc
    if stop == 'lnrows':
        def fin(w):
            w.I("gpsimd", lambda g: g.dma_start(outT[0:8, :], lraw[:, :]),
                [("pre_g", 16)], "fin.d1")
            w.I("gpsimd", lambda g: g.dma_start(outT[8:9, :], murow[0:1, :]),
                [("pre_g", 16)], "fin.d2")
            w.I("gpsimd", lambda g: g.dma_start(outT[9:10, :], rstdrow[0:1, :]),
                [("pre_g", 16)], "fin.d3")
            w.W("gpsimd", "pre_g", "fin.d3")
        w.run_block(fin)
        return nc
    if stop == 'rtg':
        def fin(w):
            w.I("gpsimd", lambda g: g.dma_start(
                outT[0:8, :], rtg[:, :]), [("pre_g", 16)], "fin.dump")
            w.W("gpsimd", "pre_g", "fin.dump")
        w.run_block(fin)
        return nc
    scope_a2.close()
    scope_a12.close()

    scope_b = ExitStack()
    incs_bf = scope_b.enter_context(nc.sbuf_tensor("incs_bf", [8, D], BF16))
    outcs_bf = scope_b.enter_context(nc.sbuf_tensor("outcs_bf", [8, D], BF16))
    yT = scope_b.enter_context(nc.sbuf_tensor("yT", [128, NK * TH], BF16))
    r2_sb = scope_b.enter_context(nc.sbuf_tensor("r2", [128, NK * TH], BF16))
    wb_bf = [scope_b.enter_context(nc.sbuf_tensor(f"wb{i}", [128, NK * OS], BF16))
             for i in range(3)]
    x1y = [scope_b.enter_context(nc.sbuf_tensor(f"x1y{i}", [128, TH], BF16))
           for i in range(2)]
    x1res = [scope_b.enter_context(nc.sbuf_tensor(f"x1res{i}", [128, TH], BF16))
             for i in range(2)]
    osc_sb = [scope_b.enter_context(nc.sbuf_tensor(f"osc{i}", [128, 512], BF16))
              for i in range(2)]
    ftmp = [scope_b.enter_context(nc.sbuf_tensor("ftmp0", [128, 512], F32))] * 2
    sq2 = [scope_b.enter_context(nc.sbuf_tensor(f"sq2_{i}", [128, 512], BF16))
           for i in range(2)]
    out_sb = [scope_b.enter_context(nc.sbuf_tensor(f"out{i}", [128, 512], F32))
              for i in range(3)]
    ln2r = scope_b.enter_context(nc.sbuf_tensor("ln2r", [1, 3 * 512], F32))
    bc3_sb = scope_b.enter_context(nc.sbuf_tensor("bc3_sb", [128, 512], BF16))
    bc4_sb = scope_b.enter_context(nc.sbuf_tensor("bc4_sb", [128, 512], BF16))
    bc5_sb = scope_b.enter_context(nc.sbuf_tensor("bc5_sb", [128, 512], BF16))
    bc6_sb = scope_b.enter_context(nc.sbuf_tensor("bc6_sb", [128, 512], BF16))
    ln2bf = scope_b.enter_context(nc.sbuf_tensor("ln2bf", [1, 2 * 512], BF16))

    def phase_b(w):
        w.I("scalar", lambda s: s.preload_act_table(AF.Sqrt), [("sc", 1)])
        w.W("gpsimd", "x1a", f"a2.x1st.{NCH-2}")
        w.W("gpsimd", "x1b", f"a2.x1st.{NCH-1}")
        w.I("gpsimd", lambda g: g.dma_start(incs_bf[:, :], incs), [("pre_g", 16)], "b.incs")
        w.I("gpsimd", lambda g: g.dma_start(outcs_bf[:, :], outcs), [("pre_g", 16)], "b.outcs")
        w.W("sync", "x1a", f"a2.x1st.{NCH-2}")
        w.W("sync", "x1b", f"a2.x1st.{NCH-1}")

        grp_seq = []

        def ln2_rows(w, HH, c5, lnbf_guard=None):
            MU, VAR, TMP = 0, 512, 1024
            w.W("vector", "mm", f"b.st.{HH}.{NOCH-1}.{c5}")
            w.I("vector", lambda v, c5=c5: v.tensor_scalar_mul(
                ln2r[0:1, MU:MU + 512], ps[4 + c5][0:1, 0:TC5], 1.0 / D), [("ve", 1)])
            w.I("vector", lambda v, c5=c5: v.tensor_scalar_mul(
                ln2r[0:1, VAR:VAR + 512], ps[6 + c5][0:1, 0:TC5], 1.0 / D),
                [("ve", 1)], f"b.l2a.{HH}.{c5}")
            w.I("vector", lambda v: v.tensor_tensor(
                ln2r[0:1, TMP:TMP + 512], ln2r[0:1, MU:MU + 512],
                ln2r[0:1, MU:MU + 512], Op.mult), [("ve", 1)])
            w.I("vector", lambda v: v.tensor_tensor(
                ln2r[0:1, VAR:VAR + 512], ln2r[0:1, VAR:VAR + 512],
                ln2r[0:1, TMP:TMP + 512], Op.subtract), [("ve", 1)])
            w.I("vector", lambda v: v.tensor_scalar(
                ln2r[0:1, VAR:VAR + 512], ln2r[0:1, VAR:VAR + 512], 1e-5, None, Op.add),
                [("ve", 1)])
            w.I("vector", lambda v: v.reciprocal(ln2r[0:1, TMP:TMP + 512],
                                                 ln2r[0:1, VAR:VAR + 512]),
                [("ve", 1)], f"b.l2rv.{HH}.{c5}")
            w.W("scalar", "ve", f"b.l2rv.{HH}.{c5}")
            w.I("scalar", lambda s: s.sqrt(ln2r[0:1, VAR:VAR + 512],
                                           ln2r[0:1, TMP:TMP + 512]),
                [("sc", 1)], f"b.l2rstd.{HH}.{c5}")
            w.W("vector", "sc", f"b.l2rstd.{HH}.{c5}")
            w.I("vector", lambda v: v.tensor_tensor(
                ln2r[0:1, TMP:TMP + 512], ln2r[0:1, MU:MU + 512],
                ln2r[0:1, VAR:VAR + 512], Op.mult), [("ve", 1)])
            if lnbf_guard is not None:
                w.W("vector", "mm", lnbf_guard)
            w.I("vector", lambda v: v.tensor_copy(
                ln2bf[0:1, 0:512], ln2r[0:1, VAR:VAR + 512]), [("ve", 1)])
            w.I("vector", lambda v: v.tensor_copy(
                ln2bf[0:1, 512:1024], ln2r[0:1, TMP:TMP + 512]),
                [("ve", 1)], f"b.l2bf.{HH}.{c5}")

        def ln2_bc(w, HH, c5, osc_guard, bcpair, bc_guard=None):
            w.W("tensor", "ve", f"b.l2bf.{HH}.{c5}")
            w.W("tensor", "sc", osc_guard)
            w.I("tensor", lambda t: t.matmul(
                ps[2][:, 0:TC5], ones1[:, 0:128], ln2bf[0:1, 0:512],
                start=True, stop=True), [("mm", 1)], f"b.l2bcA.{HH}.{c5}")
            w.W("scalar", "mm", f"b.l2bcA.{HH}.{c5}")
            if bc_guard is not None:
                w.W("scalar", "gp", bc_guard)
            w.I("scalar", lambda s: s.copy(bcpair[0][:, :], ps[2][:, 0:TC5]),
                [("sc", 1)], f"b.l2evA.{HH}.{c5}")
            w.W("tensor", "sc", f"b.l2evA.{HH}.{c5}")
            w.I("tensor", lambda t: t.matmul(
                ps[2][:, 0:TC5], ones1[:, 0:128], ln2bf[0:1, 512:1024],
                start=True, stop=True), [("mm", 1)], f"b.l2bc.{HH}.{c5}")
            w.W("scalar", "mm", f"b.l2bc.{HH}.{c5}")
            w.I("scalar", lambda s: s.copy(bcpair[1][:, :], ps[2][:, 0:TC5]),
                [("sc", 1)], f"b.l2ev.{HH}.{c5}")

        def ln2_group(w, HH, c5, g, bcpair, eng="gpsimd"):
            esem = "gp" if eng == "gpsimd" else "ve"
            gg = len(grp_seq)
            if gg >= 2:
                pHH, pc5, pg = grp_seq[gg - 2]
                w.W(eng, "oa" if gg % 2 == 0 else "ob", f"b.out.{pHH}.{pc5}.{pg}")
            grp_seq.append((HH, c5, g))
            w.W(eng, "sc", f"b.l2ev.{HH}.{c5}")
            bcA = bass.AP(bcpair[0], 0, [[512, 128], [1, TC5]])
            bcB = bass.AP(bcpair[1], 0, [[512, 128], [1, TC5]])
            r2v = r2_sb[:, TH * g + TC5 * c5: TH * g + TC5 * (c5 + 1)]
            ov = out_sb[gg % 2][:, :]
            w.I(eng, lambda g_, ov=ov, r2v=r2v, bcA=bcA: g_.tensor_tensor(
                ov, r2v, bcA, Op.mult), [(esem, 1)] if eng == "gpsimd" else (),
                f"b.normtA.{HH}.{c5}.{g}")
            if eng == "gpsimd":
                w.W(eng, "gp", f"b.normtA.{HH}.{c5}.{g}")
            w.I(eng, lambda g_, ov=ov, bcB=bcB: g_.tensor_tensor(
                ov, ov, bcB, Op.subtract), [(esem, 1)] if eng == "gpsimd" else [],
                f"b.normt.{HH}.{c5}.{g}")
            w.W("sync", esem, f"b.normt.{HH}.{c5}.{g}")
            w.I("sync", lambda s, HH=HH, c5=c5, g=g, gg=gg: s.dma_start(
                outT[128 * g: 128 * (g + 1),
                     TH * HH + TC5 * c5: TH * HH + TC5 * (c5 + 1)],
                out_sb[gg % 2][:, :]),
                [("oa" if gg % 2 == 0 else "ob", 16)], f"b.out.{HH}.{c5}.{g}")

        WBS = ("wba", "wbb", "wbc")

        def wb_fetch(w, H, pre):
            if pre >= 3:
                w.W("gpsimd", "mm", f"b.ffn.{H}.{pre-3}.1")
            elif H == 1:
                w.W("gpsimd", "mm", f"b.ffn.0.{NOCH-1}.1")
            w.I("gpsimd", lambda g, pre=pre: g.dma_start(
                wb_bf[pre % 3][:].rearrange("p (K o) -> p K o", K=NK),
                wb3[:, :, OS * pre: OS * (pre + 1)].rearrange("K p o -> p K o")),
                [(WBS[pre % 3], 16)], f"b.wb.{H}.{pre}")

        def wb_sign(w, H, pre):
            w.W("vector", WBS[pre % 3], f"b.wb.{H}.{pre}")
            if pre >= 3:
                w.W("vector", "mm", f"b.ffn.{H}.{pre-3}.1")
            w.I("vector", lambda v, pre=pre: v.tensor_scalar(
                wb_bf[pre % 3][:, :], wb_bf[pre % 3][:, :], 0.0, 0.5,
                Op.is_gt, Op.subtract), [("ve", 1)], f"b.sign.{H}.{pre}")

        for H in range(2):
            for k in range(NK):
                if H == 1 and k == 0:
                    w.W("sync", "mm", f"b.ffn.0.{NOCH-1}.1")
                if k >= 2:
                    w.W("sync", "ve", f"b.y.{H}.1.{k-2}")
                w.I("sync", lambda s, H=H, k=k: s.dma_start(
                    x1y[k % 2][:, :],
                    x1sc[128 * k: 128 * (k + 1), TH * H: TH * (H + 1)]),
                    [("ya" if k % 2 == 0 else "yb", 16)], f"b.x1y.{H}.{k}")
                if H == 1 and k == 0:
                    w.W("tensor", "mm", f"b.ffn.0.{NOCH-1}.1")
                    w.W("tensor", "sc", f"b.oscsb.0.{NOCH-1}.1")
                    w.W("tensor", "ve", f"b.r2.0.{NOCH-1}.1")
                if k >= 2:
                    w.W("tensor", "ve", f"b.y.{H}.1.{k-2}")
                if H == 0 and k == 0:
                    w.W("tensor", "pre_g", "b.outcs")
                for c5 in range(2):
                    w.I("tensor", lambda t, H=H, c5=c5, k=k: t.matmul(
                        ps[2 * (k % 2) + c5][:, 0:TC5], incs_bf[:, 128 * k: 128 * (k + 1)],
                        rtg[:, TH * H + TC5 * c5: TH * H + TC5 * (c5 + 1)],
                        start=True, stop=True), [("mm", 1)], f"b.is.{H}.{c5}.{k}")
                for c5 in range(2):
                    w.W("vector", "mm", f"b.is.{H}.{c5}.{k}")
                    w.W("vector", "ya" if k % 2 == 0 else "yb", f"b.x1y.{H}.{k}")
                    w.I("vector", lambda v, c5=c5, k=k: v.tensor_tensor(
                        yT[:, TH * k + TC5 * c5: TH * k + TC5 * (c5 + 1)],
                        x1y[k % 2][:, TC5 * c5: TC5 * (c5 + 1)],
                        ps[2 * (k % 2) + c5][:, 0:TC5], Op.mult),
                        [("ve", 1)], f"b.y.{H}.{c5}.{k}")

            if H == 1:
                w.W("sync", "ve", f"b.r2.0.{NOS-2}.1")
                w.I("sync", lambda s: s.dma_start(
                    x1res[0][:, :], x1sc[0:128, TH:2 * TH]),
                    [("ra", 16)], "b.x1res.1.0")
                w.W("sync", "ve", f"b.r2.0.{NOS-1}.1")
                w.I("sync", lambda s: s.dma_start(
                    x1res[1][:, :], x1sc[128:256, TH:2 * TH]),
                    [("rb", 16)], "b.x1res.1.1")
                wb_fetch(w, 1, 0)
                wb_fetch(w, 1, 1)
                wb_sign(w, 1, 0)
                wb_sign(w, 1, 1)
                ln2_rows(w, 0, 0)
                ln2_bc(w, 0, 0, f"b.oscsb.0.{NOCH-1}.1", (bc3_sb, bc4_sb))
                ln2_rows(w, 0, 1, lnbf_guard="b.l2bc.0.0")
                ln2_bc(w, 0, 1, "b.l2ev.0.0", (bc5_sb, bc6_sb))

            if H == 0:
                wb_fetch(w, 0, 0)
                wb_fetch(w, 0, 1)
                wb_sign(w, 0, 0)
                wb_sign(w, 0, 1)
            for osup in range(NOS):
                if H == 1:
                    ln2_group(w, 0, 0, osup, (bc3_sb, bc4_sb))
                    ln2_group(w, 0, 1, osup, (bc5_sb, bc6_sb))
                if osup + 2 < NOS:
                    wb_fetch(w, H, osup + 2)
                if not (H == 1 and osup < 2):
                    if osup >= 2:
                        w.W("sync", "ve", f"b.r2.{H}.{osup-2}.1")
                    w.I("sync", lambda s, H=H, osup=osup: s.dma_start(
                        x1res[osup % 2][:, :],
                        x1sc[128 * osup: 128 * (osup + 1), TH * H: TH * (H + 1)]),
                        [("ra" if osup % 2 == 0 else "rb", 16)], f"b.x1res.{H}.{osup}")

                och = osup
                for c5 in range(2):
                    gi = och * 2 + c5
                    FB = (0, 1, 3)
                    if gi >= 1:
                        w.W("tensor", "sc", f"b.oscsb.{H}.{(gi-1)//2}.{(gi-1)%2}")
                    else:
                        w.W("tensor", "ve", f"b.y.{H}.1.{NK-1}")
                        if H == 1:
                            w.W("tensor", "sc", "b.l2ev.0.1")
                    w.I("tensor", lambda t, H=H, och=och, c5=c5: t.matmul(
                        ps[2][:, 0:TC5], outcs_bf[:, 128 * och: 128 * (och + 1)],
                        rtg[:, TH * H + TC5 * c5: TH * H + TC5 * (c5 + 1)],
                        start=True, stop=True), [("mm", 1)], f"b.osc.{H}.{och}.{c5}")
                    if gi >= 3:
                        w.W("tensor", "ve", f"b.r2.{H}.{(gi-3)//2}.{(gi-3)%2}")
                    w.W("tensor", "ve", f"b.sign.{H}.{osup}")
                    w.I("tensor", lambda t, osup=osup, c5=c5, gi=gi: [
                        t.matmul(ps[(0, 1, 3)[gi % 3]][:, 0:TC5],
                                 wb_bf[osup % 3][:, OS * k: OS * (k + 1)],
                                 yT[:, TH * k + TC5 * c5: TH * k + TC5 * (c5 + 1)],
                                 start=(k == 0), stop=(k == NK - 1))
                        for k in range(NK)][-1], [("mm", 1)], f"b.ffn.{H}.{och}.{c5}")
                    lag = gi - 1
                    for sgi in ([lag] if lag >= 0 else []) + ([gi] if gi == 2 * NOCH - 1 else []):
                        soch, sc5 = divmod(sgi, 2)
                        w.W("tensor", "sc", f"b.sq2.{H}.{soch}.{sc5}")
                        if soch == 0 and H == 1:
                            w.W("tensor", "ve", f"b.l2a.0.{sc5}")
                        w.I("tensor", lambda t, soch=soch, sc5=sc5: t.matmul(
                            ps[4 + sc5][0:1, 0:TC5], ones128[:, 0:1],
                            r2_sb[:, TH * soch + TC5 * sc5: TH * soch + TC5 * (sc5 + 1)],
                            start=(soch == 0), stop=(soch == NOCH - 1)), [("mm", 1)])
                        w.I("tensor", lambda t, soch=soch, sc5=sc5: t.matmul(
                            ps[6 + sc5][0:1, 0:TC5], ones128[:, 0:1],
                            sq2[(2 * soch + sc5) % 2][:, :],
                            start=(soch == 0), stop=(soch == NOCH - 1)),
                            [("mm", 1)], f"b.st.{H}.{soch}.{sc5}")

                    w.W("scalar", "mm", f"b.osc.{H}.{och}.{c5}")
                    w.I("scalar", lambda s, gi=gi: s.mul(
                        osc_sb[gi % 2][:, 0:TC5], ps[2][:, 0:TC5], 2.0),
                        [("sc", 1)], f"b.oscsb.{H}.{och}.{c5}")
                    w.W("vector", "mm", f"b.ffn.{H}.{och}.{c5}")
                    w.W("vector", "sc", f"b.oscsb.{H}.{och}.{c5}")
                    w.I("vector", lambda v, gi=gi: v.tensor_tensor(
                        ftmp[gi % 2][:, 0:TC5], ps[(0, 1, 3)[gi % 3]][:, 0:TC5],
                        osc_sb[gi % 2][:, 0:TC5], Op.mult), [("ve", 1)])
                    w.W("vector", "ra" if osup % 2 == 0 else "rb", f"b.x1res.{H}.{osup}")
                    if gi >= 2:
                        w.W("vector", "mm", f"b.st.{H}.{(gi-2)//2}.{(gi-2)%2}")
                    elif H == 1:
                        w.W("vector", "mm", f"b.st.0.{NOCH-1}.1")
                    if H == 1:
                        w.W("vector", "gp", f"b.normt.0.1.{och}")
                    w.I("vector", lambda v, och=och, c5=c5, osup=osup, gi=gi:
                        v.scalar_tensor_tensor(
                            r2_sb[:, TH * och + TC5 * c5: TH * och + TC5 * (c5 + 1)],
                            ftmp[gi % 2][:, 0:TC5], bias_sb[:, och:och + 1],
                            x1res[osup % 2][:, TC5 * c5: TC5 * (c5 + 1)],
                            Op.add, Op.add), [("ve", 1)], f"b.r2.{H}.{och}.{c5}")
                    w.W("scalar", "ve", f"b.r2.{H}.{och}.{c5}")
                    w.I("scalar", lambda s, och=och, c5=c5, gi=gi: s.square(
                        sq2[gi % 2][:, :],
                        r2_sb[:, TH * och + TC5 * c5: TH * och + TC5 * (c5 + 1)]),
                        [("sc", 1)], f"b.sq2.{H}.{och}.{c5}")
                if osup + 2 < NOS:
                    wb_sign(w, H, osup + 2)

        tail_cnt = {"vector": 0, "gpsimd": 0}
        TAILBUF = {"gpsimd": ((out_sb[0], "oa", "sync"), (out_sb[2], "oc", "sync")),
                   "vector": ((out_sb[1], "oe", "scalar"), (ftmp[0], "od", "scalar"))}

        def ln2_group_tail(w, HH, c5, g, bcpair, eng):
            esem = "gp" if eng == "gpsimd" else "ve"
            i = tail_cnt[eng]
            tail_cnt[eng] += 1
            buf, osem, oq = TAILBUF[eng][i % 2]
            if i >= 2:
                w.W(eng, osem, f"b.tout.{eng}.{i-2}")
            elif i == 0:
                w.W(eng, "oa" if eng == "gpsimd" else "ob",
                    f"b.out.0.{0 if eng == 'gpsimd' else 1}.{NOCH-1}")
            w.W(eng, "sc", f"b.l2ev.{HH}.{c5}")
            bcA = bass.AP(bcpair[0], 0, [[512, 128], [1, TC5]])
            bcB = bass.AP(bcpair[1], 0, [[512, 128], [1, TC5]])
            r2v = r2_sb[:, TH * g + TC5 * c5: TH * g + TC5 * (c5 + 1)]
            ov = buf[:, 0:512]
            w.I(eng, lambda g_, ov=ov, r2v=r2v, bcA=bcA: g_.tensor_tensor(
                ov, r2v, bcA, Op.mult), [(esem, 1)] if eng == "gpsimd" else (),
                f"b.tnormA.{eng}.{i}")
            if eng == "gpsimd":
                w.W(eng, "gp", f"b.tnormA.{eng}.{i}")
            w.I(eng, lambda g_, ov=ov, bcB=bcB: g_.tensor_tensor(
                ov, ov, bcB, Op.subtract), [(esem, 1)] if eng == "gpsimd" else [],
                f"b.tnorm.{eng}.{i}")
            w.W(oq, esem, f"b.tnorm.{eng}.{i}")
            w.I(oq, lambda s, HH=HH, c5=c5, g=g, ov=ov: s.dma_start(
                outT[128 * g: 128 * (g + 1),
                     TH * HH + TC5 * c5: TH * HH + TC5 * (c5 + 1)],
                ov), [(osem, 16)], f"b.tout.{eng}.{i}", noauto=True)

        ln2_rows(w, 1, 0, lnbf_guard="b.l2bc.0.1")
        ln2_bc(w, 1, 0, f"b.oscsb.1.{NOCH-1}.1", (bc3_sb, bc4_sb),
               bc_guard=f"b.normt.0.0.{NOCH-1}")
        ln2_rows(w, 1, 1, lnbf_guard="b.l2bc.1.0")
        ln2_bc(w, 1, 1, "b.l2ev.1.0", (bc5_sb, bc6_sb),
               bc_guard=f"b.normt.0.1.{NOCH-1}")
        for g in range(NOCH):
            ln2_group_tail(w, 1, 0, g, (bc3_sb, bc4_sb),
                           eng="vector" if g % 2 else "gpsimd")
        for g in range(NOCH):
            ln2_group_tail(w, 1, 1, g, (bc5_sb, bc6_sb),
                           eng="vector" if g % 2 else "gpsimd")

        w.Wv("sync", "oa", w.cnt["oa"])
        w.Wv("sync", "ob", w.cnt["ob"])
        w.Wv("sync", "oc", w.cnt["oc"])
        w.Wv("sync", "od", w.cnt["od"])
        w.Wv("sync", "oe", w.cnt["oe"])

    w.run_block(phase_b)
    scope_b.close()
    ctx.close()
    return nc


_SEL = np.concatenate([np.arange(512 * h, 512 * h + 128) for h in range(8)])
_NC_CACHE = []


def kernel(L0_attn_hidden_states, L0_attn_Wq, L0_attn_bq, L0_attn_Wk, L0_attn_bk,
           L0_attn_Wv, L0_attn_bv, L0_attn_Wo, L0_attn_M, L0_attn_z,
           L0_ffn_weight, L0_ffn_bias, L0_ffn_gate_w, L0_ffn_in_channel_scale,
           L0_ffn_out_channel_scale):
    x = np.ascontiguousarray(np.asarray(L0_attn_hidden_states, dtype=np.float32))
    shared = {
        "wqselT": np.ascontiguousarray(np.asarray(L0_attn_Wq, dtype=np.float32)[_SEL].T),
        "bqsel": np.ascontiguousarray(np.asarray(L0_attn_bq, dtype=np.float32)[_SEL].reshape(8, 128).T),
        "woT": np.ascontiguousarray(np.asarray(L0_attn_Wo, dtype=np.float32).T),
        "mT": np.ascontiguousarray(np.asarray(L0_attn_M, dtype=np.float32).transpose(0, 2, 1)),
        "zin": np.ascontiguousarray(np.asarray(L0_attn_z, dtype=np.float32).reshape(1024, 1)),
        "gwT": np.ascontiguousarray(np.asarray(L0_ffn_gate_w, dtype=np.float32).T),
        "sel8in": np.kron(np.eye(8, dtype=np.float32), np.ones((1, 128), dtype=np.float32)),
        "incs": np.ascontiguousarray(np.asarray(L0_ffn_in_channel_scale, dtype=np.float32)),
        "outcs": np.ascontiguousarray(np.asarray(L0_ffn_out_channel_scale, dtype=np.float32)),
        "biasc": np.ascontiguousarray(np.asarray(L0_ffn_bias, dtype=np.float32).reshape(32, 128).T),
        "wbT": np.ascontiguousarray(np.asarray(L0_ffn_weight, dtype=np.float32).T),
    }
    in_maps = []
    for c in range(8):
        m = dict(shared)
        m["xT"] = np.ascontiguousarray(x[c].T)
        in_maps.append(m)
    if not _NC_CACHE:
        _NC_CACHE.append(build_program())
    nc = _NC_CACHE[0]
    res = run_bass_kernel_spmd(nc, in_maps, core_ids=list(range(8)))
    out = np.empty((8, T, D), dtype=res.results[0]["outT"].dtype)
    for c in range(8):
        out[c] = res.results[c]["outT"].T
    return out.astype(np.float32) if out.dtype == np.float32 else out


# version 22
# speedup vs baseline: 1.0685x; 1.0685x over previous
import numpy as np
from contextlib import ExitStack

import concourse.bass as bass
import concourse.mybir as mybir
from concourse.bass_utils import run_bass_kernel_spmd
from concourse.alu_op_type import AluOpType as Op

F32 = mybir.dt.float32
BF16 = mybir.dt.bfloat16
AF = mybir.ActivationFunctionType

T = 2048
D = 4096
NK = 32
CCH = 8
E = 8
TC = 256
NCH = T // TC
TH = 1024
TC5 = 512
OS = 128
NOS = D // OS
NOCH = NK


class Walker:

    ENGINES = ("tensor", "vector", "scalar", "sync", "gpsimd")

    def __init__(self, nc, sems):
        self.nc = nc
        self.sems = sems
        self.cnt = {k: 0 for k in sems}
        self.ev = {}
        self.mode = "count"
        self.handle = None

    AUTO = {"vector": "ve", "scalar": "sc"}

    def I(self, eng, build_fn, incs=(), key=None, noauto=False):
        auto = None if noauto else self.AUTO.get(eng)
        prev = None
        extra = list(incs)
        if auto is not None:
            prev = self.cnt[auto]
            self.cnt[auto] += 1
            if key is not None:
                self.ev[(key, auto)] = self.cnt[auto]
            extra = [(s, b) for (s, b) in incs if s != auto]
        for sem, by in extra:
            self.cnt[sem] += by
            if key is not None:
                self.ev[(key, sem)] = self.cnt[sem]
        if self.mode == eng:
            if prev:
                self.handle.wait_ge(self.sems[auto], prev)
            inst = build_fn(self.handle)
            if auto is not None:
                inst = inst.then_inc(self.sems[auto], 1)
            for sem, by in extra:
                inst = inst.then_inc(self.sems[sem], by)

    def W(self, eng, sem, key):
        if self.mode == eng:
            v = self.ev.get((key, sem))
            assert v is not None, f"wait on unknown event {key}/{sem}"
            self.handle.wait_ge(self.sems[sem], v)

    def Wv(self, eng, sem, value):
        if self.mode == eng:
            self.handle.wait_ge(self.sems[sem], value)

    def run_block(self, phase_fn):
        snap = dict(self.cnt)
        self.mode = "count"
        self.handle = None
        phase_fn(self)
        final = dict(self.cnt)
        with self.nc.Block(no_gpsimd_drain=True) as block:
            for eng in self.ENGINES:
                def make(eng):
                    def _(h):
                        self.cnt = dict(snap)
                        self.mode = eng
                        self.handle = h
                        phase_fn(self)
                    return _
                getattr(block, eng)(make(eng))
        self.cnt = final
        self.mode = "count"
        self.handle = None


def build_program(stop='full'):
    nc = bass.Bass("TRN2", target_bir_lowering=False)

    xT = nc.dram_tensor("xT", [D, T], F32, kind="ExternalInput").ap()
    wqselT = nc.dram_tensor("wqselT", [D, CCH * 128], F32, kind="ExternalInput").ap()
    bqsel = nc.dram_tensor("bqsel", [128, CCH], F32, kind="ExternalInput").ap()
    woT = nc.dram_tensor("woT", [D, D], F32, kind="ExternalInput").ap()
    mT = nc.dram_tensor("mT", [E, 128, 128], F32, kind="ExternalInput").ap()
    zin = nc.dram_tensor("zin", [E * 128, 1], F32, kind="ExternalInput").ap()
    gwT = nc.dram_tensor("gwT", [D, E], F32, kind="ExternalInput").ap()
    sel8in = nc.dram_tensor("sel8in", [E, E * 128], F32, kind="ExternalInput").ap()
    incs = nc.dram_tensor("incs", [E, D], F32, kind="ExternalInput").ap()
    outcs = nc.dram_tensor("outcs", [E, D], F32, kind="ExternalInput").ap()
    biasc = nc.dram_tensor("biasc", [128, NK], F32, kind="ExternalInput").ap()
    wbT = nc.dram_tensor("wbT", [D, D], F32, kind="ExternalInput").ap()
    outT = nc.dram_tensor("outT", [D, T], F32, kind="ExternalOutput").ap()
    x1sc = nc.dram_tensor("x1sc", [D, T], BF16, kind="Internal").ap()

    xT3 = xT.rearrange("(K p) t -> K p t", p=128)
    wq3 = wqselT.rearrange("(K p) c -> K p c", p=128)
    wb3 = wbT.rearrange("(K p) o -> K p o", p=128)
    x13 = x1sc.rearrange("(K p) t -> K p t", p=128)

    ctx = ExitStack()
    sems = {}
    for name in ("mm", "ve", "sc", "gp", "pre_g", "wq", "wo", "wo2", "syc",
                 "xa", "xb", "x1a", "x1b", "wba", "wbb", "wbc",
                 "xra", "xrb", "ya", "yb", "ra", "rb", "oa", "ob",
                 "oc", "od", "oe"):
        sems[name] = ctx.enter_context(nc.semaphore(name))
    w = Walker(nc, sems)

    ps = [ctx.enter_context(nc.psum_tensor(f"ps{i}", [128, 512], F32))
          for i in range(8)]

    sb = lambda name, shape, dt: ctx.enter_context(nc.sbuf_tensor(name, shape, dt))
    bq_sb = sb("bq_sb", [128, CCH], F32)
    negbq = sb("negbq", [128, CCH], F32)
    zbd = sb("zbd", [128, E * E], BF16)
    gwstat = sb("gwstat", [128, NK * 8], BF16)
    sel8 = sb("sel8", [8, E * 128], BF16)
    bias_sb = sb("bias_sb", [128, NK], F32)
    ones128 = sb("ones128", [128, 1], BF16)
    ones1 = sb("ones1", [1, 128], BF16)
    gwsneg = sb("gwsneg", [8, 1], F32)
    rtg = sb("rtg", [8, T], BF16)

    scope_a12 = ExitStack()
    ssig = scope_a12.enter_context(nc.sbuf_tensor("ssig", [128, CCH * T], BF16))


    def _dump_and_stop(tile, rows, kchunks):
        def fin(w):
            w.I("gpsimd", lambda g: g.dma_start(
                outT.rearrange("(K p) t -> K p t", p=128)[0:kchunks]
                .rearrange("K p t -> p K t"),
                tile[:].rearrange("p (K t) -> p K t", K=kchunks)),
                [("pre_g", 16)], "fin.dump")
            w.W("gpsimd", "pre_g", "fin.dump")
        w.run_block(fin)
        return nc
    scope_a1 = ExitStack()
    wqsel_sb = scope_a1.enter_context(nc.sbuf_tensor("wqsel", [128, NK * 1024], BF16))
    x_bf = [scope_a1.enter_context(nc.sbuf_tensor(f"x_bf{i}", [128, NK * TC], BF16))
            for i in range(2)]
    sig_sb = [scope_a1.enter_context(nc.sbuf_tensor(f"sig{i}", [128, CCH * TC], BF16))
              for i in range(2)]
    tmpA = [scope_a1.enter_context(nc.sbuf_tensor(f"tmpA{i}", [128, TC], BF16))
            for i in range(2)]
    tmpB = [scope_a1.enter_context(nc.sbuf_tensor(f"tmpB{i}", [128, TC], BF16))
            for i in range(2)]
    dentmp = scope_a1.enter_context(nc.sbuf_tensor("dentmp", [8, TC], F32))
    rdenf = scope_a1.enter_context(nc.sbuf_tensor("rdenf", [8, TC], F32))
    rdenbf = scope_a1.enter_context(nc.sbuf_tensor("rdenbf", [8, TC], BF16))

    def phase_a1(w):
        w.I("scalar", lambda s: s.preload_act_table(AF.Exp), [("sc", 1)])
        w.I("vector", lambda v: v.memset(ones128[:, :], 1.0), [("ve", 1)], "pre.ones")
        w.I("vector", lambda v: v.memset(ones1[:, :], 1.0), [("ve", 1)])
        w.I("vector", lambda v: v.memset(zbd[:, :], 0.0), [("ve", 1)], "pre.zbd0")
        w.I("sync", lambda s: s.dma_start(bq_sb[:, :], bqsel), [("syc", 16)], "pre.bq")
        w.I("sync", lambda s: s.dma_start(bias_sb[:, :], biasc), [("syc", 16)], "pre.bias")
        w.W("vector", "syc", "pre.bias")
        w.I("vector", lambda v: v.tensor_scalar_mul(negbq[:, :], bq_sb[:, :], -1.0),
            [("ve", 1)], "pre.negbq")
        w.W("gpsimd", "ve", "pre.zbd0")
        for h in range(E):
            w.I("gpsimd", lambda g, h=h: g.dma_start(
                zbd[:, E * h + h: E * h + h + 1], zin[128 * h: 128 * (h + 1), 0:1]),
                [("pre_g", 16)], f"pre.z.{h}")
        for k in range(NK):
            w.I("gpsimd", lambda g, k=k: g.dma_start(
                gwstat[:, 8 * k: 8 * (k + 1)], gwT[128 * k: 128 * (k + 1), :]),
                [("pre_g", 16)], f"pre.gw.{k}")
        w.I("gpsimd", lambda g: g.dma_start(sel8[:, :], sel8in), [("pre_g", 16)], "pre.allg")
        w.I("gpsimd", lambda g: g.dma_start(
            wqsel_sb[:].rearrange("p (K c) -> p K c", K=NK),
            wq3.rearrange("K p c -> p K c")),
            [("wq", 16)], "pre.wqsel")
        w.W("tensor", "wq", "pre.wqsel")
        w.W("tensor", "pre_g", "pre.allg")
        w.W("tensor", "ve", "pre.ones")
        w.I("tensor", lambda t: [
            t.matmul(ps[2][0:8, 0:1], gwstat[:, 8 * k: 8 * (k + 1)], ones128[:, 0:1],
                     start=(k == 0), stop=(k == NK - 1))
            for k in range(NK)][-1], [("mm", 1)], "pre.gwsum")
        w.W("vector", "mm", "pre.gwsum")
        w.I("vector", lambda v: v.tensor_scalar_mul(gwsneg[:, :], ps[2][0:8, 0:1], -1.0),
            [("ve", 1)], "pre.gwsneg")

        def stage_den_a(w, cc):
            if cc >= 1:
                w.W("tensor", "ve", f"a1.rden.{cc-1}")
            w.W("tensor", "ve", f"a1.sig.{cc}.{CCH-1}")
            w.I("tensor", lambda t, cc=cc: [
                t.matmul(ps[4][0:8, 0:TC], zbd[:, E * h: E * (h + 1)],
                         sig_sb[cc % 2][:, TC * h: TC * (h + 1)],
                         start=(h == 0), stop=(h == E - 1))
                for h in range(E)][-1], [("mm", 1)], f"a1.den.{cc}")
            w.W("vector", "mm", f"a1.den.{cc}")
            w.I("vector", lambda v: v.tensor_scalar(
                dentmp[:, :], ps[4][0:8, 0:TC], 1e-6, None, Op.add), [("ve", 1)])
            w.I("vector", lambda v: v.reciprocal(rdenf[:, :], dentmp[:, :]),
                [("ve", 1)], f"a1.rden.{cc}")
            w.W("scalar", "ve", f"a1.rden.{cc}")
            if cc >= 1:
                w.W("scalar", "mm", f"a1.bc.{cc-1}.{E-1}")
            w.I("scalar", lambda s: s.copy(rdenbf[:, :], rdenf[:, :]),
                [("sc", 1)], f"a1.rdenbf.{cc}")

        def stage_den_b(w, cc):
            w.W("tensor", "sc", f"a1.rdenbf.{cc}")
            for h in range(E):
                gidx = cc * E + h
                if gidx == 0:
                    w.W("tensor", "ve", "pre.gwsneg")
                if gidx >= 2:
                    pc, ph = divmod(gidx - 2, E)
                    w.W("tensor", "ve", f"a1.ssig.{pc}.{ph}")
                w.I("tensor", lambda t, h=h: t.matmul(
                    ps[2 + h % 2][:, 0:TC], sel8[:, 128 * h: 128 * (h + 1)],
                    rdenbf[0:8, 0:TC],
                    start=True, stop=True), [("mm", 1)], f"a1.bc.{cc}.{h}")
            for h in range(E):
                w.W("vector", "mm", f"a1.bc.{cc}.{h}")
                w.I("vector", lambda v, cc=cc, h=h: v.tensor_tensor(
                    ssig[:, T * h + TC * cc: T * h + TC * (cc + 1)],
                    sig_sb[cc % 2][:, TC * h: TC * (h + 1)],
                    ps[2 + h % 2][:, 0:TC], Op.mult), [("ve", 1)], f"a1.ssig.{cc}.{h}")

        for c in range(NCH):
            if c >= 2:
                w.W("gpsimd", "mm", f"a1.q.{c-2}.{CCH-1}")
            w.I("gpsimd", lambda g, c=c: g.dma_start(
                x_bf[c % 2][:].rearrange("p (K t) -> p K t", K=NK),
                xT3[:, :, TC * c: TC * (c + 1)].rearrange("K p t -> p K t")),
                [("xa" if c % 2 == 0 else "xb", 16)], f"a1.x.{c}")

            w.W("tensor", "xa" if c % 2 == 0 else "xb", f"a1.x.{c}")
            for ct in range(CCH):
                gidx = c * CCH + ct
                if gidx >= 2:
                    pc, pct = divmod(gidx - 2, CCH)
                    w.W("tensor", "sc", f"a1.sig3.{pc}.{pct}")
                w.I("tensor", lambda t, c=c, ct=ct: [
                    t.matmul(ps[ct % 2][:, 0:TC],
                             wqsel_sb[:, 1024 * k + 128 * ct: 1024 * k + 128 * ct + 128],
                             x_bf[c % 2][:, TC * k: TC * (k + 1)],
                             start=(k == 0), stop=(k == NK - 1))
                    for k in range(NK)][-1], [("mm", 1)], f"a1.q.{c}.{ct}")
                if c >= 1 and ct == 0:
                    stage_den_a(w, c - 1)
                if c >= 1 and ct == 1:
                    stage_den_b(w, c - 1)

            for ct in range(CCH):
                gidx = c * CCH + ct
                if gidx == 0:
                    w.W("scalar", "ve", "pre.negbq")
                w.W("scalar", "mm", f"a1.q.{c}.{ct}")
                if gidx >= 2:
                    pc, pct = divmod(gidx - 2, CCH)
                    w.W("scalar", "ve", f"a1.sig.{pc}.{pct}")
                w.I("scalar", lambda s, ct=ct: s.activation(
                    tmpA[ct % 2][:, :], ps[ct % 2][:, 0:TC], AF.Relu,
                    bias=negbq[:, ct:ct + 1], scale=-1.0), [("sc", 1)])
                w.I("scalar", lambda s, ct=ct: s.activation(
                    tmpB[ct % 2][:, :], ps[ct % 2][:, 0:TC], AF.Relu,
                    bias=bq_sb[:, ct:ct + 1], scale=1.0), [("sc", 1)], f"a1.sig3.{c}.{ct}")
                w.I("scalar", lambda s, ct=ct: s.activation(
                    tmpA[ct % 2][:, :], tmpA[ct % 2][:, :], AF.Exp,
                    bias=0.0, scale=-1.0), [("sc", 1)], f"a1.sigE.{c}.{ct}")
            for ct in range(CCH):
                w.W("vector", "sc", f"a1.sigE.{c}.{ct}")
                if c >= 2 and ct == 0:
                    w.W("vector", "mm", f"a1.den.{c-2}")
                w.I("vector", lambda v, c=c, ct=ct: v.tensor_tensor(
                    sig_sb[c % 2][:, TC * ct: TC * (ct + 1)],
                    tmpA[ct % 2][:, :], tmpB[ct % 2][:, :], Op.add),
                    [("ve", 1)], f"a1.sig.{c}.{ct}")


        stage_den_a(w, NCH - 1)
        stage_den_b(w, NCH - 1)

    w.run_block(phase_a1)
    if stop == 'a1':
        return _dump_and_stop(ssig, 1024, CCH)
    scope_a1.close()

    w2_sb = scope_a12.enter_context(nc.sbuf_tensor("w2", [128, CCH * D], BF16))
    scope_w2 = ExitStack()
    mt_bf = scope_w2.enter_context(nc.sbuf_tensor("mt_bf", [128, E * 128], BF16))
    wo_buf = [scope_w2.enter_context(nc.sbuf_tensor(f"wo{i}", [128, D], BF16))
              for i in range(8)]

    def phase_w2(w):
        w.I("gpsimd", lambda g: g.dma_start(
            mt_bf[:].rearrange("p (h d) -> p h d", h=E),
            mT.rearrange("h e d -> e h d")), [("pre_g", 16)], "w2.mt")
        for h in range(E):
            for g4 in range(4):
                if h >= 2:
                    w.W("gpsimd", "mm", f"w2.mm.{h-2}.7")
                blk = 4 * h + g4
                w.I("gpsimd", lambda g, h=h, blk=blk, g4=g4: g.dma_start(
                    wo_buf[4 * (h % 2) + g4][:, :], woT[128 * blk: 128 * (blk + 1), :]),
                    [("wo" if h % 2 == 0 else "wo2", 16)], f"w2.wo.{h}.{g4}")
            w.W("tensor", "wo" if h % 2 == 0 else "wo2", f"w2.wo.{h}.3")
            if h == 0:
                w.W("tensor", "pre_g", "w2.mt")
            for ot8 in range(8):
                gidx = h * 8 + ot8
                if gidx >= 2:
                    ph, pot = divmod(gidx - 2, 8)
                    w.W("tensor", "sc", f"w2.evac.{ph}.{pot}")
                w.I("tensor", lambda t, h=h, ot8=ot8: [
                    t.matmul(ps[ot8 % 2][:, 0:512], mt_bf[:, 128 * h: 128 * (h + 1)],
                             wo_buf[4 * (h % 2) + g4][:, 512 * ot8: 512 * (ot8 + 1)],
                             start=(g4 == 0), stop=(g4 == 3))
                    for g4 in range(4)][-1], [("mm", 1)], f"w2.mm.{h}.{ot8}")
            for ot8 in range(8):
                w.W("scalar", "mm", f"w2.mm.{h}.{ot8}")
                w.I("scalar", lambda s, h=h, ot8=ot8: s.copy(
                    w2_sb[:, D * h + 512 * ot8: D * h + 512 * (ot8 + 1)],
                    ps[ot8 % 2][:, 0:512]), [("sc", 1)], f"w2.evac.{h}.{ot8}")

    w.run_block(phase_w2)
    if stop == 'w2':
        return _dump_and_stop(w2_sb, 1024, CCH)
    scope_w2.close()

    scope_a2 = ExitStack()
    lraw = scope_a2.enter_context(nc.sbuf_tensor("lraw", [8, T], F32))
    murow = scope_a2.enter_context(nc.sbuf_tensor("murow", [1, T], BF16))
    rstdrow = scope_a2.enter_context(nc.sbuf_tensor("rstdrow", [1, T], BF16))
    rowf = scope_a2.enter_context(nc.sbuf_tensor("rowf", [8, 512], F32))
    rbuf = [scope_a2.enter_context(nc.sbuf_tensor(f"rbuf{i}", [128, NK * TC], F32))
            for i in range(2)]
    x_res = [scope_a2.enter_context(nc.sbuf_tensor(f"xres{i}", [128, 4 * TC], F32))
             for i in range(2)]
    bc1_sb = scope_a2.enter_context(nc.sbuf_tensor("bc1_sb", [128, TC], BF16))
    bc2_sb = scope_a2.enter_context(nc.sbuf_tensor("bc2_sb", [128, TC], BF16))
    NBF = 3
    r1bf = [scope_a2.enter_context(nc.sbuf_tensor(f"r1bf{i}", [128, 2 * TC], BF16))
            for i in range(NBF)]
    sqbf = [scope_a2.enter_context(nc.sbuf_tensor(f"sqbf{i}", [128, 2 * TC], BF16))
            for i in range(NBF)]
    ln1r = scope_a2.enter_context(nc.sbuf_tensor("ln1r", [1, 5 * TC], F32))
    lnbf = scope_a2.enter_context(nc.sbuf_tensor("lnbf", [1, 2 * TC], BF16))
    exbf = scope_a2.enter_context(nc.sbuf_tensor("exbf", [8, 512], BF16))


    NG = 8

    def phase_a2(w):
        w.I("scalar", lambda s: s.preload_act_table(AF.Sqrt), [("sc", 1)])
        def stage_fin(w, cc):
            w.W("tensor", "ve", f"a2.rowbf.{cc}")
            if cc >= 1:
                w.W("tensor", "sc", f"a2.bcev.{cc-1}")
            w.I("tensor", lambda t: t.matmul(
                ps[6][:, 0:TC], ones1[:, 0:128], lnbf[0:1, 0:TC], start=True, stop=True),
                [("mm", 1)])
            w.I("tensor", lambda t: t.matmul(
                ps[7][:, 0:TC], ones1[:, 0:128], lnbf[0:1, TC:2 * TC], start=True, stop=True),
                [("mm", 1)], f"a2.bc.{cc}")

            w.W("scalar", "mm", f"a2.bc.{cc}")
            if cc >= 1:
                w.W("scalar", "gp", f"a2.norm.{cc-1}")
            w.I("scalar", lambda s: s.copy(bc1_sb[:, :], ps[6][:, 0:TC]), [("sc", 1)])
            w.I("scalar", lambda s: s.copy(bc2_sb[:, :], ps[7][:, 0:TC]),
                [("sc", 1)], f"a2.bcev.{cc}")

            bc1b = bass.AP(bc1_sb, 0, [[TC, 128], [0, NK], [1, TC]])
            bc2b = bass.AP(bc2_sb, 0, [[TC, 128], [0, NK], [1, TC]])
            rv = rbuf[cc % 2][:].rearrange("p (K t) -> p K t", K=NK)
            w.W("gpsimd", "sc", f"a2.bcev.{cc}")
            w.W("gpsimd", "sc", f"a2.sq.{cc}.{NK//2-1}")
            w.I("gpsimd", lambda g, rv=rv: g.tensor_tensor(rv, rv, bc1b, Op.mult),
                [("gp", 1)], f"a2.norm1.{cc}")
            w.W("gpsimd", "gp", f"a2.norm1.{cc}")
            w.I("gpsimd", lambda g, rv=rv: g.tensor_tensor(rv, rv, bc2b, Op.subtract),
                [("gp", 1)], f"a2.norm.{cc}")
            w.W("gpsimd", "gp", f"a2.norm.{cc}")
            w.I("gpsimd", lambda g, cc=cc: g.dma_start(
                x13[:, :, TC * cc: TC * (cc + 1)].rearrange("K p t -> p K t"),
                rbuf[cc % 2][:].rearrange("p (K t) -> p K t", K=NK)),
                [("x1a" if cc % 2 == 0 else "x1b", 16)], f"a2.x1st.{cc}")

        for c in range(NCH):
            for grp in range(NG):
                gidx = c * NG + grp
                if gidx >= 2:
                    pc, pg = divmod(gidx - 2, NG)
                    w.W("sync", "ve", f"a2.r1.{pc}.{4*pg+3}")
                w.I("sync", lambda s, c=c, grp=grp: s.dma_start(
                    x_res[grp % 2][:].rearrange("p (f t) -> p f t", f=4),
                    xT3[4 * grp: 4 * grp + 4, :, TC * c: TC * (c + 1)]
                    .rearrange("f p t -> p f t")),
                    [("xra" if grp % 2 == 0 else "xrb", 16)], f"a2.xres.{c}.{grp}")

            A2B = (0, 1, 3)
            for ot in range(NK):
                gidx = c * NK + ot
                if gidx >= 3:
                    pc, pot = divmod(gidx - 3, NK)
                    w.W("tensor", "ve", f"a2.r1.{pc}.{pot}")
                w.I("tensor", lambda t, c=c, ot=ot: [
                    t.matmul(ps[A2B[ot % 3]][:, 0:TC],
                             w2_sb[:, D * ch + 128 * ot: D * ch + 128 * ot + 128],
                             ssig[:, T * ch + TC * c: T * ch + TC * (c + 1)],
                             start=(ch == 0), stop=(ch == CCH - 1))
                    for ch in range(CCH)][-1], [("mm", 1)], f"a2.at.{c}.{ot}")
                if ot == 6 and c >= 1:
                    stage_fin(w, c - 1)
                STL = 7
                st_pairs = []
                if ot >= STL and (ot - STL) % 2 == 1:
                    st_pairs.append((ot - STL) // 2)
                if ot == NK - 1:
                    st_pairs.extend(range((NK - STL) // 2, NK // 2))
                for sp in st_pairs:
                    if sp == 0 and c >= 1:
                        w.W("tensor", "ve", f"a2.strd.{c-1}")
                    w.W("tensor", "sc", f"a2.sq.{c}.{sp}")
                    for sh in range(2):
                        w.I("tensor", lambda t, sp=sp, sh=sh: t.matmul(
                            ps[4][0:8, TC * sh: TC * (sh + 1)],
                            gwstat[:, 8 * (2 * sp + sh): 8 * (2 * sp + sh) + 8],
                            r1bf[sp % NBF][:, TC * sh: TC * (sh + 1)],
                            start=(sp == 0 and sh == 0),
                            stop=(sp == NK // 2 - 1 and sh == 1)), [("mm", 1)])
                    w.I("tensor", lambda t, sp=sp: t.matmul(
                        ps[2][0:1, 0:2 * TC], ones128[:, 0:1], r1bf[sp % NBF][:, :],
                        start=(sp == 0), stop=(sp == NK // 2 - 1)), [("mm", 1)])
                    w.I("tensor", lambda t, sp=sp: t.matmul(
                        ps[5][0:1, 0:2 * TC], ones128[:, 0:1], sqbf[sp % NBF][:, :],
                        start=(sp == 0), stop=(sp == NK // 2 - 1)),
                        [("mm", 1)], f"a2.st.{c}.{sp}")

            if c >= 2:
                w.W("vector", "x1a" if c % 2 == 0 else "x1b", f"a2.x1st.{c-2}")
            for ot in range(NK):
                w.W("vector", "mm", f"a2.at.{c}.{ot}")
                w.W("vector", "xra" if (ot // 4) % 2 == 0 else "xrb", f"a2.xres.{c}.{ot // 4}")
                w.I("vector", lambda v, c=c, ot=ot: v.tensor_tensor(
                    rbuf[c % 2][:, TC * ot: TC * (ot + 1)], ps[A2B[ot % 3]][:, 0:TC],
                    x_res[(ot // 4) % 2][:, TC * (ot % 4): TC * (ot % 4 + 1)], Op.add),
                    [("ve", 1)], f"a2.r1.{c}.{ot}")

            for p in range(NK // 2):
                w.W("scalar", "ve", f"a2.r1.{c}.{2*p+1}")
                pgidx = c * (NK // 2) + p
                if pgidx >= NBF:
                    pc, pp = divmod(pgidx - NBF, NK // 2)
                    w.W("scalar", "mm", f"a2.st.{pc}.{pp}")
                w.I("scalar", lambda s, c=c, p=p: s.copy(
                    r1bf[p % NBF][:, :], rbuf[c % 2][:, 2 * TC * p: 2 * TC * (p + 1)]),
                    [("sc", 1)])
                w.I("scalar", lambda s, p=p: s.square(
                    sqbf[p % NBF][:, :], r1bf[p % NBF][:, :]), [("sc", 1)], f"a2.sq.{c}.{p}")

            MU, VAR, TMP, NU, RSD = 0, TC, 2 * TC, 3 * TC, 4 * TC
            w.W("scalar", "mm", f"a2.st.{c}.{NK//2-1}")
            w.I("scalar", lambda s: s.mul(
                ln1r[0:1, MU:MU + TC], ps[2][0:1, TC:2 * TC], 1.0 / D), [("sc", 1)])
            w.I("scalar", lambda s: s.mul(
                ln1r[0:1, VAR:VAR + TC], ps[5][0:1, TC:2 * TC], 1.0 / D), [("sc", 1)])
            w.I("scalar", lambda s, c=c: s.copy(
                lraw[:, TC * c: TC * (c + 1)], ps[4][0:8, TC:2 * TC]),
                [("sc", 1)], f"a2.stev.{c}")
            w.W("vector", "sc", f"a2.stev.{c}")
            w.I("vector", lambda v: v.scalar_tensor_tensor(
                ln1r[0:1, MU:MU + TC], ps[2][0:1, 0:TC], 1.0 / D,
                ln1r[0:1, MU:MU + TC], Op.mult, Op.add), [("ve", 1)])
            w.I("vector", lambda v: v.scalar_tensor_tensor(
                ln1r[0:1, VAR:VAR + TC], ps[5][0:1, 0:TC], 1.0 / D,
                ln1r[0:1, VAR:VAR + TC], Op.mult, Op.add), [("ve", 1)])
            w.I("vector", lambda v, c=c: v.tensor_tensor(
                lraw[:, TC * c: TC * (c + 1)], ps[4][0:8, 0:TC],
                lraw[:, TC * c: TC * (c + 1)], Op.add), [("ve", 1)], f"a2.strd.{c}")
            w.I("vector", lambda v, c=c: v.tensor_copy(
                murow[0:1, TC * c: TC * (c + 1)], ln1r[0:1, MU:MU + TC]),
                [("ve", 1)], f"a2.mu.{c}")
            w.I("vector", lambda v: v.tensor_tensor(
                ln1r[0:1, TMP:TMP + TC], ln1r[0:1, MU:MU + TC], ln1r[0:1, MU:MU + TC],
                Op.mult), [("ve", 1)])
            w.I("vector", lambda v: v.tensor_tensor(
                ln1r[0:1, VAR:VAR + TC], ln1r[0:1, VAR:VAR + TC], ln1r[0:1, TMP:TMP + TC],
                Op.subtract), [("ve", 1)])
            w.I("vector", lambda v: v.tensor_scalar(
                ln1r[0:1, VAR:VAR + TC], ln1r[0:1, VAR:VAR + TC], 1e-5, None, Op.add),
                [("ve", 1)])
            w.I("vector", lambda v: v.reciprocal(
                ln1r[0:1, TMP:TMP + TC], ln1r[0:1, VAR:VAR + TC]),
                [("ve", 1)], f"a2.rv.{c}")
            w.W("scalar", "ve", f"a2.rv.{c}")
            w.I("scalar", lambda s: s.sqrt(
                ln1r[0:1, RSD:RSD + TC], ln1r[0:1, TMP:TMP + TC]),
                [("sc", 1)], f"a2.rstd.{c}")
            w.I("scalar", lambda s, c=c: s.copy(
                rstdrow[0:1, TC * c: TC * (c + 1)], ln1r[0:1, RSD:RSD + TC]),
                [("sc", 1)], f"a2.rstdbf.{c}")
            w.W("vector", "sc", f"a2.rstd.{c}")
            w.I("vector", lambda v: v.tensor_tensor(
                ln1r[0:1, NU:NU + TC], ln1r[0:1, MU:MU + TC],
                ln1r[0:1, RSD:RSD + TC], Op.mult),
                [("ve", 1)], f"a2.nu.{c}")
            if c >= 1:
                w.W("vector", "mm", f"a2.bc.{c-1}")
            w.I("vector", lambda v: v.tensor_copy(
                lnbf[0:1, 0:TC], ln1r[0:1, RSD:RSD + TC]), [("ve", 1)])
            w.I("vector", lambda v: v.tensor_copy(
                lnbf[0:1, TC:2 * TC], ln1r[0:1, NU:NU + TC]),
                [("ve", 1)], f"a2.rowbf.{c}")


        stage_fin(w, NCH - 1)

        w.I("scalar", lambda s: s.preload_act_table(AF.Exp), [("sc", 1)], "rt.tab")
        for rc in range(T // 512):
            cpos = 512 * rc
            cc = min(2 * rc + 1, NCH - 1)
            w.W("tensor", "ve", f"a2.mu.{cc}")
            w.W("tensor", "sc", f"a2.rstdbf.{cc}")
            if rc == 0:
                w.W("tensor", "ve", f"a2.r1.{NCH-1}.{NK-1}")
            else:
                w.W("tensor", "ve", f"rt.lg.{rc-1}")
            w.I("tensor", lambda t, cpos=cpos: t.matmul(
                ps[0][0:8, 0:512], ones1[:, 0:8], murow[0:1, cpos:cpos + 512],
                start=True, stop=True), [("mm", 1)])
            w.I("tensor", lambda t, cpos=cpos: t.matmul(
                ps[1][0:8, 0:512], ones1[:, 0:8], rstdrow[0:1, cpos:cpos + 512],
                start=True, stop=True), [("mm", 1)], f"rt.bc.{rc}")
            w.W("vector", "mm", f"rt.bc.{rc}")
            w.I("vector", lambda v, cpos=cpos: v.scalar_tensor_tensor(
                rowf[:, :], ps[0][0:8, 0:512], gwsneg[:, 0:1],
                lraw[:, cpos:cpos + 512], Op.mult, Op.add), [("ve", 1)])
            w.I("vector", lambda v: v.tensor_tensor(
                rowf[:, :], rowf[:, :], ps[1][0:8, 0:512], Op.mult),
                [("ve", 1)], f"rt.lg.{rc}")
            w.W("scalar", "ve", f"rt.lg.{rc}")
            w.I("scalar", lambda s: s.activation(exbf[:, :], rowf[:, :], AF.Exp),
                [("sc", 1)], f"rt.ex.{rc}")
            w.W("tensor", "sc", f"rt.ex.{rc}")
            w.I("tensor", lambda t: t.matmul(
                ps[2][0:1, 0:512], ones128[0:8, 0:1], exbf[:, :], start=True, stop=True),
                [("mm", 1)], f"rt.sum.{rc}")
            w.W("vector", "mm", f"rt.sum.{rc}")
            w.I("vector", lambda v: v.reciprocal(rowf[0:1, :], ps[2][0:1, 0:512]),
                [("ve", 1)], f"rt.rs.{rc}")
            w.W("scalar", "ve", f"rt.rs.{rc}")
            w.I("scalar", lambda s: s.copy(lnbf[0:1, 0:512], rowf[0:1, :]),
                [("sc", 1)], f"rt.rsbf.{rc}")
            w.W("tensor", "sc", f"rt.rsbf.{rc}")
            if rc >= 1:
                w.W("tensor", "ve", f"rt.done.{rc-1}")
            w.I("tensor", lambda t: t.matmul(
                ps[3][0:8, 0:512], ones1[:, 0:8], lnbf[0:1, 0:512], start=True, stop=True),
                [("mm", 1)], f"rt.bcr.{rc}")
            w.W("vector", "mm", f"rt.bcr.{rc}")
            w.I("vector", lambda v, cpos=cpos: v.tensor_tensor(
                rtg[:, cpos:cpos + 512], exbf[:, :], ps[3][0:8, 0:512], Op.mult),
                [("ve", 1)], f"rt.done.{rc}")

    w.run_block(phase_a2)
    if stop == 'a2':
        def fin(w):
            w.W("gpsimd", "x1a", f"a2.x1st.{NCH-2}")
            w.W("gpsimd", "x1b", f"a2.x1st.{NCH-1}")
            w.I("gpsimd", lambda g: g.dma_start(outT, x1sc), [("pre_g", 16)], "fin.dump")
            w.W("gpsimd", "pre_g", "fin.dump")
        w.run_block(fin)
        return nc
    if stop == 'lnrows':
        def fin(w):
            w.I("gpsimd", lambda g: g.dma_start(outT[0:8, :], lraw[:, :]),
                [("pre_g", 16)], "fin.d1")
            w.I("gpsimd", lambda g: g.dma_start(outT[8:9, :], murow[0:1, :]),
                [("pre_g", 16)], "fin.d2")
            w.I("gpsimd", lambda g: g.dma_start(outT[9:10, :], rstdrow[0:1, :]),
                [("pre_g", 16)], "fin.d3")
            w.W("gpsimd", "pre_g", "fin.d3")
        w.run_block(fin)
        return nc
    if stop == 'rtg':
        def fin(w):
            w.I("gpsimd", lambda g: g.dma_start(
                outT[0:8, :], rtg[:, :]), [("pre_g", 16)], "fin.dump")
            w.W("gpsimd", "pre_g", "fin.dump")
        w.run_block(fin)
        return nc
    scope_a2.close()
    scope_a12.close()

    scope_b = ExitStack()
    incs_bf = scope_b.enter_context(nc.sbuf_tensor("incs_bf", [8, D], BF16))
    outcs_bf = scope_b.enter_context(nc.sbuf_tensor("outcs_bf", [8, D], BF16))
    yT = scope_b.enter_context(nc.sbuf_tensor("yT", [128, NK * TH], BF16))
    r2_sb = scope_b.enter_context(nc.sbuf_tensor("r2", [128, NK * TH], BF16))
    wb_bf = [scope_b.enter_context(nc.sbuf_tensor(f"wb{i}", [128, NK * OS], BF16))
             for i in range(3)]
    x1y = [scope_b.enter_context(nc.sbuf_tensor(f"x1y{i}", [128, TH], BF16))
           for i in range(2)]
    x1res = [scope_b.enter_context(nc.sbuf_tensor(f"x1res{i}", [128, TH], BF16))
             for i in range(2)]
    osc_sb = [scope_b.enter_context(nc.sbuf_tensor(f"osc{i}", [128, 512], BF16))
              for i in range(2)]
    ftmp = [scope_b.enter_context(nc.sbuf_tensor("ftmp0", [128, 512], F32))] * 2
    sq2 = [scope_b.enter_context(nc.sbuf_tensor(f"sq2_{i}", [128, 512], BF16))
           for i in range(2)]
    out_sb = [scope_b.enter_context(nc.sbuf_tensor(f"out{i}", [128, 512], F32))
              for i in range(3)]
    ln2r = scope_b.enter_context(nc.sbuf_tensor("ln2r", [1, 3 * 512], F32))
    bc3_sb = scope_b.enter_context(nc.sbuf_tensor("bc3_sb", [128, 512], BF16))
    bc4_sb = scope_b.enter_context(nc.sbuf_tensor("bc4_sb", [128, 512], BF16))
    bc5_sb = scope_b.enter_context(nc.sbuf_tensor("bc5_sb", [128, 512], BF16))
    bc6_sb = scope_b.enter_context(nc.sbuf_tensor("bc6_sb", [128, 512], BF16))
    ln2bf = scope_b.enter_context(nc.sbuf_tensor("ln2bf", [1, 2 * 512], BF16))

    def phase_b(w):
        w.I("scalar", lambda s: s.preload_act_table(AF.Sqrt), [("sc", 1)])
        w.W("gpsimd", "x1a", f"a2.x1st.{NCH-2}")
        w.W("gpsimd", "x1b", f"a2.x1st.{NCH-1}")
        w.I("gpsimd", lambda g: g.dma_start(incs_bf[:, :], incs), [("pre_g", 16)], "b.incs")
        w.I("gpsimd", lambda g: g.dma_start(outcs_bf[:, :], outcs), [("pre_g", 16)], "b.outcs")
        w.W("sync", "x1a", f"a2.x1st.{NCH-2}")
        w.W("sync", "x1b", f"a2.x1st.{NCH-1}")

        grp_seq = []

        def ln2_rows(w, HH, c5, lnbf_guard=None):
            MU, VAR, TMP = 0, 512, 1024
            w.W("vector", "mm", f"b.st.{HH}.{NOCH-1}.{c5}")
            w.I("vector", lambda v, c5=c5: v.tensor_scalar_mul(
                ln2r[0:1, MU:MU + 512], ps[4 + c5][0:1, 0:TC5], 1.0 / D), [("ve", 1)])
            w.I("vector", lambda v, c5=c5: v.tensor_scalar_mul(
                ln2r[0:1, VAR:VAR + 512], ps[6 + c5][0:1, 0:TC5], 1.0 / D),
                [("ve", 1)], f"b.l2a.{HH}.{c5}")
            w.I("vector", lambda v: v.tensor_tensor(
                ln2r[0:1, TMP:TMP + 512], ln2r[0:1, MU:MU + 512],
                ln2r[0:1, MU:MU + 512], Op.mult), [("ve", 1)])
            w.I("vector", lambda v: v.tensor_tensor(
                ln2r[0:1, VAR:VAR + 512], ln2r[0:1, VAR:VAR + 512],
                ln2r[0:1, TMP:TMP + 512], Op.subtract), [("ve", 1)])
            w.I("vector", lambda v: v.tensor_scalar(
                ln2r[0:1, VAR:VAR + 512], ln2r[0:1, VAR:VAR + 512], 1e-5, None, Op.add),
                [("ve", 1)])
            w.I("vector", lambda v: v.reciprocal(ln2r[0:1, TMP:TMP + 512],
                                                 ln2r[0:1, VAR:VAR + 512]),
                [("ve", 1)], f"b.l2rv.{HH}.{c5}")
            w.W("scalar", "ve", f"b.l2rv.{HH}.{c5}")
            w.I("scalar", lambda s: s.sqrt(ln2r[0:1, VAR:VAR + 512],
                                           ln2r[0:1, TMP:TMP + 512]),
                [("sc", 1)], f"b.l2rstd.{HH}.{c5}")
            w.W("vector", "sc", f"b.l2rstd.{HH}.{c5}")
            w.I("vector", lambda v: v.tensor_tensor(
                ln2r[0:1, TMP:TMP + 512], ln2r[0:1, MU:MU + 512],
                ln2r[0:1, VAR:VAR + 512], Op.mult), [("ve", 1)])
            if lnbf_guard is not None:
                w.W("vector", "mm", lnbf_guard)
            w.I("vector", lambda v: v.tensor_copy(
                ln2bf[0:1, 0:512], ln2r[0:1, VAR:VAR + 512]), [("ve", 1)])
            w.I("vector", lambda v: v.tensor_copy(
                ln2bf[0:1, 512:1024], ln2r[0:1, TMP:TMP + 512]),
                [("ve", 1)], f"b.l2bf.{HH}.{c5}")

        def ln2_bc(w, HH, c5, osc_guard, bcpair, bc_guard=None):
            w.W("tensor", "ve", f"b.l2bf.{HH}.{c5}")
            w.W("tensor", "sc", osc_guard)
            w.I("tensor", lambda t: t.matmul(
                ps[2][:, 0:TC5], ones1[:, 0:128], ln2bf[0:1, 0:512],
                start=True, stop=True), [("mm", 1)], f"b.l2bcA.{HH}.{c5}")
            w.W("scalar", "mm", f"b.l2bcA.{HH}.{c5}")
            if bc_guard is not None:
                w.W("scalar", "gp", bc_guard)
            w.I("scalar", lambda s: s.copy(bcpair[0][:, :], ps[2][:, 0:TC5]),
                [("sc", 1)], f"b.l2evA.{HH}.{c5}")
            w.W("tensor", "sc", f"b.l2evA.{HH}.{c5}")
            w.I("tensor", lambda t: t.matmul(
                ps[2][:, 0:TC5], ones1[:, 0:128], ln2bf[0:1, 512:1024],
                start=True, stop=True), [("mm", 1)], f"b.l2bc.{HH}.{c5}")
            w.W("scalar", "mm", f"b.l2bc.{HH}.{c5}")
            w.I("scalar", lambda s: s.copy(bcpair[1][:, :], ps[2][:, 0:TC5]),
                [("sc", 1)], f"b.l2ev.{HH}.{c5}")

        def ln2_group(w, HH, c5, g, bcpair, eng="gpsimd"):
            esem = "gp" if eng == "gpsimd" else "ve"
            gg = len(grp_seq)
            if gg >= 2:
                pHH, pc5, pg = grp_seq[gg - 2]
                w.W(eng, "oa" if gg % 2 == 0 else "ob", f"b.out.{pHH}.{pc5}.{pg}")
            grp_seq.append((HH, c5, g))
            w.W(eng, "sc", f"b.l2ev.{HH}.{c5}")
            bcA = bass.AP(bcpair[0], 0, [[512, 128], [1, TC5]])
            bcB = bass.AP(bcpair[1], 0, [[512, 128], [1, TC5]])
            r2v = r2_sb[:, TH * g + TC5 * c5: TH * g + TC5 * (c5 + 1)]
            ov = out_sb[gg % 2][:, :]
            w.I(eng, lambda g_, ov=ov, r2v=r2v, bcA=bcA: g_.tensor_tensor(
                ov, r2v, bcA, Op.mult), [(esem, 1)] if eng == "gpsimd" else (),
                f"b.normtA.{HH}.{c5}.{g}")
            if eng == "gpsimd":
                w.W(eng, "gp", f"b.normtA.{HH}.{c5}.{g}")
            w.I(eng, lambda g_, ov=ov, bcB=bcB: g_.tensor_tensor(
                ov, ov, bcB, Op.subtract), [(esem, 1)] if eng == "gpsimd" else [],
                f"b.normt.{HH}.{c5}.{g}")
            w.W("sync", esem, f"b.normt.{HH}.{c5}.{g}")
            w.I("sync", lambda s, HH=HH, c5=c5, g=g, gg=gg: s.dma_start(
                outT[128 * g: 128 * (g + 1),
                     TH * HH + TC5 * c5: TH * HH + TC5 * (c5 + 1)],
                out_sb[gg % 2][:, :]),
                [("oa" if gg % 2 == 0 else "ob", 16)], f"b.out.{HH}.{c5}.{g}")

        WBS = ("wba", "wbb", "wbc")

        def wb_fetch(w, H, pre):
            if pre >= 3:
                w.W("gpsimd", "mm", f"b.ffn.{H}.{pre-3}.1")
            elif H == 1:
                w.W("gpsimd", "mm", f"b.ffn.0.{NOCH-1}.1")
            w.I("gpsimd", lambda g, pre=pre: g.dma_start(
                wb_bf[pre % 3][:].rearrange("p (K o) -> p K o", K=NK),
                wb3[:, :, OS * pre: OS * (pre + 1)].rearrange("K p o -> p K o")),
                [(WBS[pre % 3], 16)], f"b.wb.{H}.{pre}")

        def wb_sign(w, H, pre):
            w.W("vector", WBS[pre % 3], f"b.wb.{H}.{pre}")
            if pre >= 3:
                w.W("vector", "mm", f"b.ffn.{H}.{pre-3}.1")
            w.I("vector", lambda v, pre=pre: v.tensor_scalar(
                wb_bf[pre % 3][:, :], wb_bf[pre % 3][:, :], 0.0, 0.5,
                Op.is_gt, Op.subtract), [("ve", 1)], f"b.sign.{H}.{pre}")

        for H in range(2):
            for k in range(NK):
                if H == 1 and k == 0:
                    w.W("sync", "mm", f"b.ffn.0.{NOCH-1}.1")
                if k >= 2:
                    w.W("sync", "ve", f"b.y.{H}.1.{k-2}")
                w.I("sync", lambda s, H=H, k=k: s.dma_start(
                    x1y[k % 2][:, :],
                    x1sc[128 * k: 128 * (k + 1), TH * H: TH * (H + 1)]),
                    [("ya" if k % 2 == 0 else "yb", 16)], f"b.x1y.{H}.{k}")
                if H == 1 and k == 0:
                    w.W("tensor", "mm", f"b.ffn.0.{NOCH-1}.1")
                    w.W("tensor", "sc", f"b.oscsb.0.{NOCH-1}.1")
                    w.W("tensor", "ve", f"b.r2.0.{NOCH-1}.1")
                if k >= 2:
                    w.W("tensor", "ve", f"b.y.{H}.1.{k-2}")
                if H == 0 and k == 0:
                    w.W("tensor", "pre_g", "b.outcs")
                for c5 in range(2):
                    w.I("tensor", lambda t, H=H, c5=c5, k=k: t.matmul(
                        ps[2 * (k % 2) + c5][:, 0:TC5], incs_bf[:, 128 * k: 128 * (k + 1)],
                        rtg[:, TH * H + TC5 * c5: TH * H + TC5 * (c5 + 1)],
                        start=True, stop=True), [("mm", 1)], f"b.is.{H}.{c5}.{k}")
                for c5 in range(2):
                    w.W("vector", "mm", f"b.is.{H}.{c5}.{k}")
                    w.W("vector", "ya" if k % 2 == 0 else "yb", f"b.x1y.{H}.{k}")
                    w.I("vector", lambda v, c5=c5, k=k: v.tensor_tensor(
                        yT[:, TH * k + TC5 * c5: TH * k + TC5 * (c5 + 1)],
                        x1y[k % 2][:, TC5 * c5: TC5 * (c5 + 1)],
                        ps[2 * (k % 2) + c5][:, 0:TC5], Op.mult),
                        [("ve", 1)], f"b.y.{H}.{c5}.{k}")

            if H == 1:
                w.W("sync", "ve", f"b.r2.0.{NOS-2}.1")
                w.I("sync", lambda s: s.dma_start(
                    x1res[0][:, :], x1sc[0:128, TH:2 * TH]),
                    [("ra", 16)], "b.x1res.1.0")
                w.W("sync", "ve", f"b.r2.0.{NOS-1}.1")
                w.I("sync", lambda s: s.dma_start(
                    x1res[1][:, :], x1sc[128:256, TH:2 * TH]),
                    [("rb", 16)], "b.x1res.1.1")
                wb_fetch(w, 1, 0)
                wb_fetch(w, 1, 1)
                wb_sign(w, 1, 0)
                wb_sign(w, 1, 1)
                ln2_rows(w, 0, 0)
                ln2_bc(w, 0, 0, f"b.oscsb.0.{NOCH-1}.1", (bc3_sb, bc4_sb))
                ln2_rows(w, 0, 1, lnbf_guard="b.l2bc.0.0")
                ln2_bc(w, 0, 1, "b.l2ev.0.0", (bc5_sb, bc6_sb))

            if H == 0:
                wb_fetch(w, 0, 0)
                wb_fetch(w, 0, 1)
                wb_sign(w, 0, 0)
                wb_sign(w, 0, 1)
            for osup in range(NOS):
                if H == 1:
                    ln2_group(w, 0, 0, osup, (bc3_sb, bc4_sb))
                    ln2_group(w, 0, 1, osup, (bc5_sb, bc6_sb))
                if osup + 2 < NOS:
                    wb_fetch(w, H, osup + 2)
                if not (H == 1 and osup < 2):
                    if osup >= 2:
                        w.W("sync", "ve", f"b.r2.{H}.{osup-2}.1")
                    w.I("sync", lambda s, H=H, osup=osup: s.dma_start(
                        x1res[osup % 2][:, :],
                        x1sc[128 * osup: 128 * (osup + 1), TH * H: TH * (H + 1)]),
                        [("ra" if osup % 2 == 0 else "rb", 16)], f"b.x1res.{H}.{osup}")

                och = osup
                for c5 in range(2):
                    gi = och * 2 + c5
                    FB = (0, 1, 3)
                    if gi >= 1:
                        w.W("tensor", "sc", f"b.oscsb.{H}.{(gi-1)//2}.{(gi-1)%2}")
                    else:
                        w.W("tensor", "ve", f"b.y.{H}.1.{NK-1}")
                        if H == 1:
                            w.W("tensor", "sc", "b.l2ev.0.1")
                    w.I("tensor", lambda t, H=H, och=och, c5=c5: t.matmul(
                        ps[2][:, 0:TC5], outcs_bf[:, 128 * och: 128 * (och + 1)],
                        rtg[:, TH * H + TC5 * c5: TH * H + TC5 * (c5 + 1)],
                        start=True, stop=True), [("mm", 1)], f"b.osc.{H}.{och}.{c5}")
                    if gi >= 3:
                        w.W("tensor", "ve", f"b.r2.{H}.{(gi-3)//2}.{(gi-3)%2}")
                    w.W("tensor", "ve", f"b.sign.{H}.{osup}")
                    w.I("tensor", lambda t, osup=osup, c5=c5, gi=gi: [
                        t.matmul(ps[(0, 1, 3)[gi % 3]][:, 0:TC5],
                                 wb_bf[osup % 3][:, OS * k: OS * (k + 1)],
                                 yT[:, TH * k + TC5 * c5: TH * k + TC5 * (c5 + 1)],
                                 start=(k == 0), stop=(k == NK - 1))
                        for k in range(NK)][-1], [("mm", 1)], f"b.ffn.{H}.{och}.{c5}")
                    lag = gi - 1
                    for sgi in ([lag] if lag >= 0 else []) + ([gi] if gi == 2 * NOCH - 1 else []):
                        soch, sc5 = divmod(sgi, 2)
                        w.W("tensor", "sc", f"b.sq2.{H}.{soch}.{sc5}")
                        if soch == 0 and H == 1:
                            w.W("tensor", "ve", f"b.l2a.0.{sc5}")
                        w.I("tensor", lambda t, soch=soch, sc5=sc5: t.matmul(
                            ps[4 + sc5][0:1, 0:TC5], ones128[:, 0:1],
                            r2_sb[:, TH * soch + TC5 * sc5: TH * soch + TC5 * (sc5 + 1)],
                            start=(soch == 0), stop=(soch == NOCH - 1)), [("mm", 1)])
                        w.I("tensor", lambda t, soch=soch, sc5=sc5: t.matmul(
                            ps[6 + sc5][0:1, 0:TC5], ones128[:, 0:1],
                            sq2[(2 * soch + sc5) % 2][:, :],
                            start=(soch == 0), stop=(soch == NOCH - 1)),
                            [("mm", 1)], f"b.st.{H}.{soch}.{sc5}")

                    w.W("scalar", "mm", f"b.osc.{H}.{och}.{c5}")
                    w.I("scalar", lambda s, gi=gi: s.mul(
                        osc_sb[gi % 2][:, 0:TC5], ps[2][:, 0:TC5], 2.0),
                        [("sc", 1)], f"b.oscsb.{H}.{och}.{c5}")
                    w.W("vector", "mm", f"b.ffn.{H}.{och}.{c5}")
                    w.W("vector", "sc", f"b.oscsb.{H}.{och}.{c5}")
                    w.I("vector", lambda v, gi=gi: v.tensor_tensor(
                        ftmp[gi % 2][:, 0:TC5], ps[(0, 1, 3)[gi % 3]][:, 0:TC5],
                        osc_sb[gi % 2][:, 0:TC5], Op.mult), [("ve", 1)])
                    w.W("vector", "ra" if osup % 2 == 0 else "rb", f"b.x1res.{H}.{osup}")
                    if gi >= 2:
                        w.W("vector", "mm", f"b.st.{H}.{(gi-2)//2}.{(gi-2)%2}")
                    elif H == 1:
                        w.W("vector", "mm", f"b.st.0.{NOCH-1}.1")
                    if H == 1:
                        w.W("vector", "gp", f"b.normt.0.1.{och}")
                    w.I("vector", lambda v, och=och, c5=c5, osup=osup, gi=gi:
                        v.scalar_tensor_tensor(
                            r2_sb[:, TH * och + TC5 * c5: TH * och + TC5 * (c5 + 1)],
                            ftmp[gi % 2][:, 0:TC5], bias_sb[:, och:och + 1],
                            x1res[osup % 2][:, TC5 * c5: TC5 * (c5 + 1)],
                            Op.add, Op.add), [("ve", 1)], f"b.r2.{H}.{och}.{c5}")
                    w.W("scalar", "ve", f"b.r2.{H}.{och}.{c5}")
                    w.I("scalar", lambda s, och=och, c5=c5, gi=gi: s.square(
                        sq2[gi % 2][:, :],
                        r2_sb[:, TH * och + TC5 * c5: TH * och + TC5 * (c5 + 1)]),
                        [("sc", 1)], f"b.sq2.{H}.{och}.{c5}")
                if osup + 2 < NOS:
                    wb_sign(w, H, osup + 2)

        tail_cnt = {"vector": 0, "gpsimd": 0}
        TAILBUF = {"gpsimd": ((out_sb[0], "oa", "sync"), (out_sb[2], "oc", "sync")),
                   "vector": ((out_sb[1], "oe", "scalar"), (ftmp[0], "od", "scalar"))}

        def ln2_group_tail(w, HH, c5, g, bcpair, eng):
            esem = "gp" if eng == "gpsimd" else "ve"
            i = tail_cnt[eng]
            tail_cnt[eng] += 1
            buf, osem, oq = TAILBUF[eng][i % 2]
            if i >= 2:
                w.W(eng, osem, f"b.tout.{eng}.{i-2}")
            elif i == 0:
                w.W(eng, "oa" if eng == "gpsimd" else "ob",
                    f"b.out.0.{0 if eng == 'gpsimd' else 1}.{NOCH-1}")
            w.W(eng, "sc", f"b.l2ev.{HH}.{c5}")
            bcA = bass.AP(bcpair[0], 0, [[512, 128], [1, TC5]])
            bcB = bass.AP(bcpair[1], 0, [[512, 128], [1, TC5]])
            r2v = r2_sb[:, TH * g + TC5 * c5: TH * g + TC5 * (c5 + 1)]
            ov = buf[:, 0:512]
            w.I(eng, lambda g_, ov=ov, r2v=r2v, bcA=bcA: g_.tensor_tensor(
                ov, r2v, bcA, Op.mult), [(esem, 1)] if eng == "gpsimd" else (),
                f"b.tnormA.{eng}.{i}")
            if eng == "gpsimd":
                w.W(eng, "gp", f"b.tnormA.{eng}.{i}")
            w.I(eng, lambda g_, ov=ov, bcB=bcB: g_.tensor_tensor(
                ov, ov, bcB, Op.subtract), [(esem, 1)] if eng == "gpsimd" else [],
                f"b.tnorm.{eng}.{i}")
            w.W(oq, esem, f"b.tnorm.{eng}.{i}")
            w.I(oq, lambda s, HH=HH, c5=c5, g=g, ov=ov: s.dma_start(
                outT[128 * g: 128 * (g + 1),
                     TH * HH + TC5 * c5: TH * HH + TC5 * (c5 + 1)],
                ov), [(osem, 16)], f"b.tout.{eng}.{i}", noauto=True)

        ln2_rows(w, 1, 0, lnbf_guard="b.l2bc.0.1")
        ln2_bc(w, 1, 0, f"b.oscsb.1.{NOCH-1}.1", (bc3_sb, bc4_sb),
               bc_guard=f"b.normt.0.0.{NOCH-1}")
        ln2_rows(w, 1, 1, lnbf_guard="b.l2bc.1.0")
        ln2_bc(w, 1, 1, "b.l2ev.1.0", (bc5_sb, bc6_sb),
               bc_guard=f"b.normt.0.1.{NOCH-1}")
        for g in range(NOCH):
            ln2_group_tail(w, 1, 0, g, (bc3_sb, bc4_sb),
                           eng="vector" if g % 2 else "gpsimd")
        for g in range(NOCH):
            ln2_group_tail(w, 1, 1, g, (bc5_sb, bc6_sb),
                           eng="vector" if g % 2 else "gpsimd")

        w.Wv("sync", "oa", w.cnt["oa"])
        w.Wv("sync", "ob", w.cnt["ob"])
        w.Wv("sync", "oc", w.cnt["oc"])
        w.Wv("sync", "od", w.cnt["od"])
        w.Wv("sync", "oe", w.cnt["oe"])

    w.run_block(phase_b)
    scope_b.close()
    ctx.close()
    return nc


_SEL = np.concatenate([np.arange(512 * h, 512 * h + 128) for h in range(8)])
_NC_CACHE = []


def kernel(L0_attn_hidden_states, L0_attn_Wq, L0_attn_bq, L0_attn_Wk, L0_attn_bk,
           L0_attn_Wv, L0_attn_bv, L0_attn_Wo, L0_attn_M, L0_attn_z,
           L0_ffn_weight, L0_ffn_bias, L0_ffn_gate_w, L0_ffn_in_channel_scale,
           L0_ffn_out_channel_scale):
    x = np.ascontiguousarray(np.asarray(L0_attn_hidden_states, dtype=np.float32))
    shared = {
        "wqselT": np.ascontiguousarray(np.asarray(L0_attn_Wq, dtype=np.float32)[_SEL].T),
        "bqsel": np.ascontiguousarray(np.asarray(L0_attn_bq, dtype=np.float32)[_SEL].reshape(8, 128).T),
        "woT": np.ascontiguousarray(np.asarray(L0_attn_Wo, dtype=np.float32).T),
        "mT": np.ascontiguousarray(np.asarray(L0_attn_M, dtype=np.float32).transpose(0, 2, 1)),
        "zin": np.ascontiguousarray(np.asarray(L0_attn_z, dtype=np.float32).reshape(1024, 1)),
        "gwT": np.ascontiguousarray(np.asarray(L0_ffn_gate_w, dtype=np.float32).T),
        "sel8in": np.kron(np.eye(8, dtype=np.float32), np.ones((1, 128), dtype=np.float32)),
        "incs": np.ascontiguousarray(np.asarray(L0_ffn_in_channel_scale, dtype=np.float32)),
        "outcs": np.ascontiguousarray(np.asarray(L0_ffn_out_channel_scale, dtype=np.float32)),
        "biasc": np.ascontiguousarray(np.asarray(L0_ffn_bias, dtype=np.float32).reshape(32, 128).T),
        "wbT": np.ascontiguousarray(np.asarray(L0_ffn_weight, dtype=np.float32).T),
    }
    in_maps = []
    for c in range(8):
        m = dict(shared)
        m["xT"] = np.ascontiguousarray(x[c].T)
        in_maps.append(m)
    if not _NC_CACHE:
        _NC_CACHE.append(build_program())
    nc = _NC_CACHE[0]
    res = run_bass_kernel_spmd(nc, in_maps, core_ids=list(range(8)))
    out = np.empty((8, T, D), dtype=res.results[0]["outT"].dtype)
    for c in range(8):
        out[c] = res.results[c]["outT"].T
    return out.astype(np.float32) if out.dtype == np.float32 else out
